# version 25
# baseline (speedup 1.0000x reference)
import os
import numpy as np
import ml_dtypes
import concourse.bass as bass
import concourse.mybir as mybir
from concourse.bass_utils import run_bass_kernel_spmd

F32 = mybir.dt.float32
F32R = mybir.dt.float32r
BF16 = mybir.dt.bfloat16
I32 = mybir.dt.int32
AF = mybir.ActivationFunctionType
ALU = mybir.AluOpType
AX = mybir.AxisListType

E = 1024
SCALING = float(128 ** (-0.5))
INV127 = float(np.float32(1.0) / np.float32(127.0))
PI = float(np.pi)
TWO_PI = float(2 * np.pi)
INV2PI = float(1.0 / (2 * np.pi))

ENGS = ("sync", "tensor", "vector", "scalar", "gpsimd")


class Sched:
    def __init__(self):
        self.phases = []

    def phase(self, name, after=None):
        ph = {"name": name, "work": {e: [] for e in ENGS}, "after": after,
              "idx": len(self.phases)}
        self.phases.append(ph)
        return ph

    def on(self, ph, eng, fn, dmas=0, compute=None, dmasem=None):
        if compute is None:
            compute = (dmas == 0)
        ph["work"][eng].append((fn, dmas, compute, dmasem))

    def emit(self, block, esems, dsem):
        totE = {e: [] for e in ENGS}
        totD = []
        for ph in self.phases:
            for e in ENGS:
                totE[e].append(sum(1 for (_, _, c, _) in ph["work"][e] if c))
            totD.append(sum(nd for e in ENGS for (_, nd, _, ds) in ph["work"][e] if ds is None))
        cumE = {e: np.cumsum([0] + totE[e]) for e in ENGS}
        cumD = np.cumsum([0] + totD)

        def make_section(eng_name):
            def section(eng):
                class Ctx:
                    pass
                ctx = Ctx()
                ctx.dma = lambda out, in_: eng.dma_start(out=out, in_=in_).then_inc(dsem, 16)
                lastE = {e: 0 for e in ENGS}
                lastD = 0
                for k, ph in enumerate(self.phases):
                    lst = ph["work"][eng_name]
                    if not lst:
                        continue
                    after = ph["after"]
                    semwaits = []
                    if after is None or after.get("all"):
                        reqs = [(e, int(cumE[e][k])) for e in ENGS]
                        reqD = int(cumD[k])
                        if after is not None:
                            semwaits = after.get("sem", [])
                    else:
                        reqs = []
                        reqD = 0
                        for e, dep_ph in after.items():
                            if e == "dma":
                                reqD = int(cumD[dep_ph["idx"] + 1])
                            elif e == "sem":
                                semwaits = dep_ph
                            else:
                                reqs.append((e, int(cumE[e][dep_ph["idx"] + 1])))
                    for e, needE in reqs:
                        if needE > lastE[e]:
                            eng.wait_ge(esems[e], needE)
                            lastE[e] = needE
                    if reqD > lastD:
                        eng.wait_ge(dsem, 16 * reqD)
                        lastD = reqD
                    for (s, v) in semwaits:
                        eng.wait_ge(s, v)
                    for (fn, nd, comp, ds) in lst:
                        if ds is not None:
                            ctx.dma = lambda out, in_, ds=ds: eng.dma_start(out=out, in_=in_).then_inc(ds, 16)
                        else:
                            ctx.dma = lambda out, in_: eng.dma_start(out=out, in_=in_).then_inc(dsem, 16)
                        last = fn(eng, ctx)
                        if comp:
                            assert last is not None, f"phase {ph['name']} on {eng_name} returned None"
                            last.then_inc(esems[eng_name], 1)
            return section

        for name in ENGS:
            getattr(block, name)(make_section(name))
        return sum(int(cumE[e][-1]) for e in ENGS), int(cumD[-1])


def host_consts():
    c = {}
    c["c_eye_f32"] = np.eye(128, dtype=np.float32)
    oh = np.zeros((24, 128, 24), np.float32)
    for cc in range(24):
        oh[cc, :, cc] = 1.0
    c["c_oh"] = oh.reshape(24 * 128, 24)
    e4 = np.zeros((8, 4, 8), np.float32)
    for i in range(8):
        e4[i, :, i] = 1.0
    c["c_e4pat"] = e4.reshape(8, 32)
    bm = np.zeros((128, 16, 8), np.float32)
    for t in range(16):
        bm[t * 8:(t + 1) * 8, t, :] = 1.0
    c["c_blockmask"] = bm.reshape(128, 128)
    s16 = np.zeros((16, 128), np.float32)
    for p in range(128):
        s16[p % 16, p] = 1.0
    c["c_s16"] = s16
    return c


def build_nc(tokens, T, tap=None):
    ntiles = tokens // T
    nsub = T // 128
    NG = T // 16
    NR = NG // 4
    assert tokens % T == 0 and T % 128 == 0 and NG % 4 == 0

    nc = bass.Bass("TRN2", target_bir_lowering=False)

    x_in = nc.dram_tensor("x", [tokens, E], F32, kind="ExternalInput").ap()
    wqkv = nc.dram_tensor("wqkv", [3072, E], F32, kind="ExternalInput").ap()
    bqkv = nc.dram_tensor("bqkv", [24, 128], F32, kind="ExternalInput").ap()
    wout = nc.dram_tensor("wout", [E, E], F32, kind="ExternalInput").ap()
    bout = nc.dram_tensor("bout", [8, 128], F32, kind="ExternalInput").ap()
    qwrow = nc.dram_tensor("qwrow", [1, 256], F32, kind="ExternalInput").ap()
    pp = nc.dram_tensor("pp", [16, E], F32, kind="ExternalInput").ap()
    gaterow = nc.dram_tensor("gaterow", [1, 96], F32, kind="ExternalInput").ap()
    logrow = nc.dram_tensor("logrow", [1, 4], F32, kind="ExternalInput").ap()
    wg = nc.dram_tensor("wg", [3072, E], I32, kind="ExternalInput").ap()
    wu = nc.dram_tensor("wu", [3072, E], I32, kind="ExternalInput").ap()
    wd = nc.dram_tensor("wd", [E, 3072], I32, kind="ExternalInput").ap()
    consts_np = host_consts()
    cin = {}
    for k, v in consts_np.items():
        dt = BF16 if v.dtype == ml_dtypes.bfloat16 else F32
        cin[k] = nc.dram_tensor(k, list(v.shape), dt, kind="ExternalInput").ap()

    y_out = nc.dram_tensor("y", [tokens, E], F32, kind="ExternalOutput").ap()
    dbg = nc.dram_tensor("dbg", [128, 4096 if tap else 4], F32, kind="ExternalOutput").ap()

    s_wqkv = nc.dram_tensor("s_wqkv", [128, 24 * 8 * 128], BF16).ap()
    s_wout = nc.dram_tensor("s_wout", [128, 8 * 8 * 128], BF16).ap()
    s_wg = nc.dram_tensor("s_wg", [128, 24 * 8 * 128], BF16).ap()
    s_wu = nc.dram_tensor("s_wu", [128, 24 * 8 * 128], BF16).ap()
    s_wd = nc.dram_tensor("s_wd", [128, 24 * 8 * 128], BF16).ap()
    s_rn = nc.dram_tensor("s_rn", [1, 24 * T], F32).ap()

    sch = Sched()
    import contextlib
    with contextlib.ExitStack() as stack:
        T8 = 8 * T
        T16 = 16 * T
        regs = {}
        off = 0
        def R(name, cols):
            nonlocal off
            regs[name] = (off, cols)
            off += cols
        R("x_tm", T8)
        R("qbR", 24 * T)
        R("stA", T16)
        R("stAk", T16)
        R("stB", T16)
        R("stBk", T16)
        R("aux", 2 * T)
        regs["krhs"] = regs["stA"]
        AFC = off
        arF = stack.enter_context(nc.sbuf_tensor("arF", [128, AFC], F32))
        zsq = stack.enter_context(nc.sbuf_tensor("zsq", [128, 24 * T], F32R))
        zve = stack.enter_context(nc.sbuf_tensor("zve", [128, 16 * T], F32R))
        zxf = stack.enter_context(nc.sbuf_tensor("zxf", [128, 8 * T], F32R))

        hregs = {"x_bf": (0, T8), "meas": (T8, T8), "x2bf": (2 * T8, T8), "hbuf": (3 * T8, T8)}
        hregs["P"] = (0, 7 * T)
        hregs["Q"] = (T8, 7 * T)
        hregs["P2"] = (2 * T8, 7 * T)
        hregs["Q2"] = (3 * T8, 7 * T)
        AHC = 4 * T8
        arH = stack.enter_context(nc.sbuf_tensor("arH", [128, AHC], BF16))

        wslot = stack.enter_context(nc.sbuf_tensor("wslot", [128, 8192], BF16))
        diag = stack.enter_context(nc.sbuf_tensor("diag", [128, 63 * 128], BF16))
        eyeF = stack.enter_context(nc.sbuf_tensor("eyeF", [128, 128], F32))
        ohR = stack.enter_context(nc.sbuf_tensor("ohR", [128, 24 * 24], F32R))
        eyeR = stack.enter_context(nc.sbuf_tensor("eyeR", [128, 128], F32R))
        onescR = stack.enter_context(nc.sbuf_tensor("onescR", [128, 1], F32R))
        onesrR = stack.enter_context(nc.sbuf_tensor("onesrR", [1, 128], F32R))
        cosrow = stack.enter_context(nc.sbuf_tensor("cosrow", [1, 128], F32R))
        sinrow = stack.enter_context(nc.sbuf_tensor("sinrow", [1, 128], F32R))
        e4R = stack.enter_context(nc.sbuf_tensor("e4R", [8, 32], F32R))
        bmask = stack.enter_context(nc.sbuf_tensor("bmask", [128, 128], F32))
        bqkv_sb = stack.enter_context(nc.sbuf_tensor("bqkv_sb", [128, 24], F32))
        bout_sb = stack.enter_context(nc.sbuf_tensor("bout_sb", [128, 8], F32))
        supw_sb = stack.enter_context(nc.sbuf_tensor("supw_sb", [128, 4], F32))
        kbias = stack.enter_context(nc.sbuf_tensor("kbias", [128, 3], F32))
        thin = stack.enter_context(nc.sbuf_tensor("thin", [1, 1792], F32))
        PS = stack.enter_context(nc.psum_tensor([128, 4096], F32))

        esems = {e: stack.enter_context(nc.semaphore(f"esem_{e}")) for e in ENGS}
        dsem = stack.enter_context(nc.semaphore("dsem"))
        wsem = [stack.enter_context(nc.semaphore("wsemA")), stack.enter_context(nc.semaphore("wsemB"))]
        xsem = stack.enter_context(nc.semaphore("xsem"))
        osem = stack.enter_context(nc.semaphore("osem"))
        rsem = stack.enter_context(nc.semaphore("rsem"))

        print(f"[sbuf] arF={AFC*4//1024}KB arH={AHC*2//1024}KB wslot=16KB diag=16KB "
              f"misc~{(128*4*4 + 576*4 + 2048*4 + 512*4 + 24*512*4)//1024 + 2}KB")

        def V(name, o2=0, n=None):
            o, cols = regs[name]
            if n is None:
                n = cols - o2
            return arF[:, o + o2: o + o2 + n]

        def HV(name, o2=0, n=None):
            o, cols = hregs[name]
            if n is None:
                n = cols - o2
            return arH[:, o + o2: o + o2 + n]

        def rawF(name, o2, dims):
            o, _ = regs[name]
            return bass.AP(arF, o + o2, [[AFC, 128]] + dims)

        def rawH(name, o2, dims):
            o, _ = hregs[name]
            return bass.AP(arH, o + o2, [[AHC, 128]] + dims)

        def r32(apx):
            return apx.bitcast(F32R)

        def f32(apx):
            return apx.bitcast(F32)

        ph = sch.phase("cl")
        def f_cl(eng, ctx):
            ctx.dma(eyeF[:], cin["c_eye_f32"][:])
            ctx.dma(bmask[:], cin["c_blockmask"][:])
            for cc in range(24):
                ctx.dma(V("qbR", cc * 24, 24), cin["c_oh"][cc * 128:(cc + 1) * 128, :])
            ctx.dma(V("qbR", 1024, 128)[0:16, :], cin["c_s16"][:])
            ctx.dma(V("qbR", 1152, 32)[0:8, :], cin["c_e4pat"][:])
            ctx.dma(V("qbR", 1184, 128)[0:16, :], pp[:, 0:128])
            ctx.dma(thin[0:1, 0:96], gaterow[:])
            ctx.dma(thin[0:1, 96:352], qwrow[:])
            ctx.dma(thin[0:1, 352:356], logrow[:])
            ctx.dma(V("qbR", 1312, 128)[0:24, :], bqkv[:])
            return ctx.dma(V("qbR", 1440, 128)[0:8, :], bout[:])
        sch.on(ph, "sync", f_cl, dmas=34, compute=False)

        ph = sch.phase("cm0")
        def f_cm0(eng, ctx):
            eng.memset(V("qbR", 576, 1), 1.0)
            eng.memset(kbias[:, 0:1], 1e-12)
            eng.memset(kbias[:, 1:2], 1e-5)
            eng.memset(kbias[:, 2:3], 2e-26)
            eng.memset(V("qbR", 577, 128)[0:1, :], 1.0)
            return eng.memset(thin[0:1, 558:560], 0.0)
        sch.on(ph, "vector", f_cm0)
        ph = sch.phase("cr")
        def f_cr(eng, ctx):
            eng.tensor_copy(out=ohR[:], in_=V("qbR", 0, 576))
            eng.tensor_copy(out=eyeR[:], in_=eyeF[:])
            eng.tensor_copy(out=onescR[:], in_=V("qbR", 576, 1))
            eng.tensor_copy(out=zsq[0:1, 191:192], in_=kbias[0:1, 0:1])
            return eng.tensor_copy(out=onesrR[:], in_=V("qbR", 577, 128)[0:1, :])
        sch.on(ph, "vector", f_cr)

        ph = sch.phase("prep")
        def f_rep(eng, ctx):
            return eng.matmul(PS[:, 0:128], V("qbR", 1024, 128)[0:16, :], V("qbR", 1184, 128)[0:16, :],
                              start=True, stop=True)
        sch.on(ph, "tensor", f_rep)
        ph = sch.phase("pmask")
        def f_pm(eng, ctx):
            return eng.tensor_tensor(out=zsq[:, 0:128], in0=PS[:, 0:128], in1=eyeF[:], op=ALU.mult)
        sch.on(ph, "vector", f_pm)
        ph = sch.phase("phirow")
        def f_pr(eng, ctx):
            return eng.matmul(PS[0:1, 512:640], onescR[:], zsq[:, 0:128], start=True, stop=True)
        sch.on(ph, "tensor", f_pr)

        def vphase(name, fn):
            ph2 = sch.phase(name)
            sch.on(ph2, "vector", fn)
        g3r = lambda: thin[0:1, 0:96].rearrange("p (d g f) -> p d g f", d=3, f=4)
        vphase("ang1", lambda eng, ctx: (
            eng.tensor_copy(out=thin[0:1, 384:512], in_=PS[0:1, 512:640]),
            eng.tensor_single_scalar(out=thin[0:1, 512:533].rearrange("p (d g) -> p d g", d=3),
                                     in_=g3r()[:, :, 0:7, 0], scalar=0.5, op=ALU.mult),
            eng.tensor_copy(out=thin[0:1, 533:554].rearrange("p (d g) -> p d g", d=3),
                            in_=g3r()[:, :, 0:7, 1]),
            eng.tensor_copy(out=thin[0:1, 554:558],
                            in_=thin[0:1, 96:352].rearrange("p (c r) -> p c r", c=4)[:, :, 0]))[-1])
        for it in range(4):
            xo = 384 if it % 2 == 0 else 736
            xn = 736 if it % 2 == 0 else 384
            def f_w1(eng, ctx, xo=xo):
                eng.tensor_single_scalar(out=thin[0:1, 560:736], in_=thin[0:1, xo:xo + 176],
                                         scalar=PI, op=ALU.is_gt)
                return eng.tensor_single_scalar(out=thin[0:1, 1088:1264], in_=thin[0:1, xo:xo + 176],
                                                scalar=-PI, op=ALU.is_lt)
            vphase("angw1", f_w1)
            vphase("angw2", lambda eng, ctx: eng.tensor_tensor(
                out=thin[0:1, 912:1088], in0=thin[0:1, 560:736], in1=thin[0:1, 1088:1264], op=ALU.subtract))
            def f_w3(eng, ctx, xo=xo, xn=xn):
                return eng.scalar_tensor_tensor(out=thin[0:1, xn:xn + 176], in0=thin[0:1, 912:1088],
                                                scalar=-TWO_PI, in1=thin[0:1, xo:xo + 176],
                                                op0=ALU.mult, op1=ALU.add)
            vphase("angw3", f_w3)
        vphase("angc1", lambda eng, ctx: eng.tensor_single_scalar(
            out=thin[0:1, 560:736], in_=thin[0:1, 384:560], scalar=PI / 2, op=ALU.add))
        vphase("angc2", lambda eng, ctx: eng.tensor_single_scalar(
            out=thin[0:1, 1088:1264], in_=thin[0:1, 560:736], scalar=PI, op=ALU.is_gt))
        vphase("angc3", lambda eng, ctx: eng.scalar_tensor_tensor(
            out=thin[0:1, 912:1088], in0=thin[0:1, 1088:1264], scalar=-TWO_PI,
            in1=thin[0:1, 560:736], op0=ALU.mult, op1=ALU.add))
        ph = sch.phase("trig")
        def f_trig(eng, ctx):
            eng.activation(out=thin[0:1, 1264:1440], in_=thin[0:1, 384:560], func=AF.Sin)
            return eng.activation(out=thin[0:1, 1440:1616], in_=thin[0:1, 912:1088], func=AF.Sin)
        sch.on(ph, "scalar", f_trig)

        SV, CV = 1264, 1440
        ct = lambda: thin[0:1, CV + 128:CV + 149]
        st = lambda: thin[0:1, SV + 128:SV + 149]
        cp = lambda: thin[0:1, CV + 149:CV + 170]
        sp = lambda: thin[0:1, SV + 149:SV + 170]
        vphase("coef1", lambda eng, ctx: (
            eng.tensor_copy(out=cosrow[:], in_=thin[0:1, CV:CV + 128]),
            eng.tensor_copy(out=sinrow[:], in_=thin[0:1, SV:SV + 128]),
            eng.tensor_tensor(out=thin[0:1, 1679:1700], in0=st(), in1=cp(), op=ALU.mult),
            eng.tensor_tensor(out=thin[0:1, 1637:1658], in0=st(), in1=sp(), op=ALU.mult),
            eng.tensor_scalar(out=thin[0:1, 1700:1704], in0=thin[0:1, SV + 170:SV + 174],
                              scalar1=0.3, scalar2=0.5, op0=ALU.mult, op1=ALU.add),
            eng.tensor_reduce(out=thin[0:1, 1712:1713], in_=thin[0:1, 352:356], axis=AX.X, op=ALU.max))[-1])
        vphase("coef2", lambda eng, ctx: (
            eng.tensor_tensor(out=thin[0:1, 1616:1637], in0=ct(), in1=thin[0:1, 1679:1700], op=ALU.add),
            eng.tensor_tensor(out=thin[0:1, 1658:1679], in0=ct(), in1=thin[0:1, 1679:1700], op=ALU.subtract),
            eng.tensor_scalar(out=thin[0:1, 1704:1708], in0=thin[0:1, 1700:1704], scalar1=-1.0, scalar2=1.0,
                              op0=ALU.mult, op1=ALU.add),
            eng.tensor_tensor(out=thin[0:1, 1713:1717], in0=thin[0:1, 352:356],
                              in1=thin[0:1, 1712:1713].broadcast_to([1, 4]), op=ALU.subtract))[-1])
        vphase("coef3", lambda eng, ctx: (
            eng.tensor_single_scalar(out=thin[0:1, 1616:1637], in_=thin[0:1, 1616:1637], scalar=-1.0, op=ALU.add),
            eng.tensor_single_scalar(out=thin[0:1, 1658:1679], in_=thin[0:1, 1658:1679], scalar=-1.0, op=ALU.add),
            eng.tensor_tensor(out=thin[0:1, 1708:1712], in0=thin[0:1, 1700:1704],
                              in1=thin[0:1, 1704:1708], op=ALU.mult))[-1])
        ph = sch.phase("supe")
        def f_supe(eng, ctx):
            return eng.activation(out=thin[0:1, 1717:1721], in_=thin[0:1, 1713:1717], func=AF.Exp)
        sch.on(ph, "scalar", f_supe)
        vphase("supn1", lambda eng, ctx: (
            eng.tensor_single_scalar(out=thin[0:1, 1708:1712], in_=thin[0:1, 1708:1712],
                                     scalar=SCALING, op=ALU.mult),
            eng.tensor_reduce(out=thin[0:1, 1721:1722], in_=thin[0:1, 1717:1721], axis=AX.X, op=ALU.add))[-1])
        vphase("supn2", lambda eng, ctx: eng.reciprocal(out=thin[0:1, 1722:1723], in_=thin[0:1, 1721:1722]))
        vphase("supn3", lambda eng, ctx: eng.tensor_tensor(
            out=thin[0:1, 1723:1727], in0=thin[0:1, 1717:1721],
            in1=thin[0:1, 1722:1723].broadcast_to([1, 4]), op=ALU.mult))
        vphase("supn4", lambda eng, ctx: (
            eng.tensor_copy(out=zsq[0:1, 128:149], in_=thin[0:1, 1616:1637]),
            eng.tensor_copy(out=zsq[0:1, 149:170], in_=thin[0:1, 1637:1658]),
            eng.tensor_copy(out=zsq[0:1, 170:191], in_=thin[0:1, 1658:1679]),
            eng.tensor_copy(out=zsq[0:1, 192:196], in_=thin[0:1, 1708:1712]),
            eng.tensor_copy(out=zsq[0:1, 196:200], in_=thin[0:1, 1723:1727]))[-1])

        ph = sch.phase("repl")
        def f_repl(eng, ctx):
            eng.matmul(PS[:, 0:64], onesrR[:], zsq[0:1, 128:192], start=True, stop=True)
            eng.matmul(PS[0:8, 512:516], onesrR[0:1, 0:8], zsq[0:1, 192:196], start=True, stop=True)
            eng.matmul(PS[:, 1024:1028], onesrR[:], zsq[0:1, 196:200], start=True, stop=True)
            eng.transpose(PS[:, 1536:1560], V("qbR", 1312, 128)[0:24, :], eyeF[0:24, 0:24])
            return eng.transpose(PS[:, 2048:2056], V("qbR", 1440, 128)[0:8, :], eyeF[0:8, 0:8])
        sch.on(ph, "tensor", f_repl)
        ph = sch.phase("repl2")
        def f_repl2(eng, ctx):
            eng.tensor_copy(out=V("qbR", 1700, 63), in_=PS[:, 0:63])
            eng.tensor_copy(out=supw_sb[:], in_=PS[:, 1024:1028])
            eng.tensor_copy(out=bqkv_sb[:], in_=PS[:, 1536:1560])
            eng.tensor_copy(out=bout_sb[:], in_=PS[:, 2048:2056])
            a8 = PS[0:8, 512:516].unsqueeze(2).broadcast_to([8, 4, 8])
            return eng.tensor_tensor(out=e4R[:].rearrange("p (c i) -> p c i", c=4),
                                     in0=V("qbR", 1152, 32)[0:8, :].rearrange("p (c i) -> p c i", c=4),
                                     in1=a8, op=ALU.mult)
        sch.on(ph, "vector", f_repl2)
        ph = sch.phase("diagb")
        def f_diag(eng, ctx):
            last = None
            for j in range(63):
                last = eng.tensor_scalar(out=diag[:, j * 128:(j + 1) * 128], in0=eyeF[:],
                                         scalar1=V("qbR", 1700 + j, 1), scalar2=None, op0=ALU.mult)
            return last
        sch.on(ph, "vector", f_diag)

        def wconv(srcT, scratch, nrb, nech, colfn, isint=False):
            for rb in range(nrb):
                ph1 = sch.phase("wv")
                def f_load(eng, ctx, rb=rb, srcT=srcT, nech=nech, isint=isint):
                    dst = V("stA", 0, nech * 128)
                    if isint:
                        dst = dst.bitcast(I32)
                    return ctx.dma(dst, srcT[rb * 128:(rb + 1) * 128, :])
                sch.on(ph1, "sync", f_load, dmas=1, compute=False)
                if isint:
                    ph2 = sch.phase("wv")
                    def f_cast(eng, ctx, nech=nech):
                        return eng.tensor_copy(out=V("stAk", 0, nech * 128),
                                               in_=V("stA", 0, nech * 128).bitcast(I32))
                    sch.on(ph2, "vector", f_cast)
                    srcreg = "stAk"
                else:
                    srcreg = "stA"
                ph3 = sch.phase("wv")
                def f_tr(eng, ctx, nech=nech, srcreg=srcreg):
                    last = None
                    for k in range(nech):
                        last = eng.transpose(PS[:, k * 128:(k + 1) * 128],
                                             V(srcreg, k * 128, 128), eyeF[:])
                    return last
                sch.on(ph3, "tensor", f_tr)
                ph4 = sch.phase("wv")
                def f_ev(eng, ctx, nech=nech):
                    return eng.copy(out=arH[:, 4096:4096 + nech * 128], in_=PS[:, 0:nech * 128])
                sch.on(ph4, "scalar", f_ev)
                ph5 = sch.phase("wv")
                def f_st(eng, ctx, rb=rb, nech=nech, scratch=scratch, colfn=colfn):
                    last = None
                    for k in range(nech):
                        last = ctx.dma(scratch[:, colfn(rb, k):colfn(rb, k) + 128],
                                       arH[:, 4096 + k * 128:4096 + (k + 1) * 128])
                    return last
                sch.on(ph5, "sync", f_st, dmas=nech, compute=False)

        wconv(wqkv, s_wqkv, 24, 8, lambda m, k: (m * 8 + k) * 128)
        wconv(wout, s_wout, 8, 8, lambda m, k: (m * 8 + k) * 128)
        wconv(wg, s_wg, 24, 8, lambda m, k: (m * 8 + k) * 128, isint=True)
        wconv(wu, s_wu, 24, 8, lambda m, k: (m * 8 + k) * 128, isint=True)
        wconv(wd, s_wd, 8, 24, lambda m, k: (k * 8 + m) * 128, isint=True)

        def vph(name, fn, after=None):
            p2 = sch.phase(name, after=after)
            sch.on(p2, "vector", fn)
            return p2

        def sph(name, fn, after=None):
            p2 = sch.phase(name, after=after)
            sch.on(p2, "scalar", fn)
            return p2

        def tph(name, fn, after=None):
            p2 = sch.phase(name, after=after)
            sch.on(p2, "tensor", fn)
            return p2

        def gph(name, fn, after=None):
            p2 = sch.phase(name, after=after)
            sch.on(p2, "gpsimd", fn)
            return p2

        def dph(name, fn, ndma, after=None, sem=None):
            p2 = sch.phase(name, after=after)
            sch.on(p2, "sync", fn, dmas=ndma, compute=False, dmasem=sem)
            return p2

        def tapdump(name, apfn, tile_i):
            if tap and tap[0] == tile_i and tap[1] == name:
                ph = sch.phase(f"tap_{name}")
                def f_tap(eng, ctx):
                    apx = apfn()
                    return ctx.dma(dbg[0:apx.partition_size(), 0:apx.free_size()], apx)
                sch.on(ph, "sync", f_tap, dmas=1, compute=False)
                sch.phase("tapb")

        last_pre = sch.phases[-1]
        slot_user = [None, None]
        nfetch = 0
        nwv = [0, 0]
        nx = 0
        no = 0
        nr = 0
        prev_ost = None
        prev_oev = None
        prev_otr = None

        rnrow = lambda c: (zve[0:1, c * T:(c + 1) * T] if c < 16
                           else zsq[0:1, 4096 + (c - 16) * T: 4096 + (c - 15) * T])

        def make_part1(ti, deps):
            t0 = ti * T
            H = {"qe": {}, "qs": {}}

            def th_load():
                nonlocal nx
                def f_xl(eng, ctx, t0=t0):
                    last = None
                    for s in range(nsub):
                        last = ctx.dma(V("qbR", 4096 + s * 1024, 1024),
                                       x_in[t0 + s * 128: t0 + (s + 1) * 128, :])
                    return last
                H["xl"] = dph("xload", f_xl, nsub, sem=xsem, after=deps.get("xl"))
                nx += nsub

            def th_xtr(w):
                def f_xt(eng, ctx, w=w):
                    last = None
                    for k in range(4):
                        for s in range(nsub):
                            idx = k * nsub + s
                            last = eng.transpose(PS[:, 1024 + idx * 128:1024 + idx * 128 + 128],
                                                 V("qbR", 4096 + s * 1024 + (w * 4 + k) * 128, 128),
                                                 eyeF[:])
                    return last
                xtdep = {"sem": [(xsem, 16 * nx)]}
                if w == 0:
                    if deps.get("xtr_sc") is not None:
                        xtdep["scalar"] = deps["xtr_sc"]
                    if deps.get("xtr_v") is not None:
                        xtdep["vector"] = deps["xtr_v"]
                else:
                    xtdep["vector"] = H["xe0"]
                H[f"xt{w}"] = tph("xtr", f_xt, after=xtdep)
                xe_dep = {"tensor": H[f"xt{w}"]}
                if deps.get(f"xe{w}_v") is not None:
                    xe_dep["vector"] = deps[f"xe{w}_v"]
                H[f"xe{w}"] = vph("xev", lambda eng, ctx, w=w: eng.tensor_copy(
                    out=zxf[:, w * 1024:(w + 1) * 1024], in_=PS[:, 1024:2048]),
                    after=xe_dep)

            def th_cast():
                H["xc"] = sph("xcast", lambda eng, ctx: eng.copy(out=HV("x_bf", 0, T8), in_=f32(zxf[:])),
                              after={"vector": H["xe1"]})

            def th_qkv(g):
                nonlocal nfetch
                sl = nfetch % 2
                nfetch += 1
                def f_ww(eng, ctx, g=g, sl=sl):
                    return ctx.dma(wslot[:, sl * 4096:sl * 4096 + 4096], s_wqkv[:, g * 4096:(g + 1) * 4096])
                wdep = None if slot_user[sl] is None else {"tensor": slot_user[sl]}
                dph("qkvw", f_ww, 1, after=wdep, sem=wsem[sl])
                nwv[sl] += 1
                for half in range(2):
                    sub = g * 2 + half
                    pc = 1024 + (sub % 2) * 512
                    mmdep = {"sem": [(wsem[sl], 16 * nwv[sl])],
                             "scalar": H["qs"].get(sub - 2, H["xc"]),
                             "vector": H["qe"].get(sub - 2, H["xe1"])}
                    def f_qm(eng, ctx, sl=sl, half=half, pc=pc):
                        last = None
                        for ml in range(2):
                            mo = half * 2 + ml
                            for k in range(8):
                                last = eng.matmul(PS[:, pc + ml * 256: pc + ml * 256 + T],
                                                  wslot[:, sl * 4096 + (mo * 8 + k) * 128: sl * 4096 + (mo * 8 + k) * 128 + 128],
                                                  HV("x_bf", k * T, T), start=(k == 0), stop=(k == 7))
                        return last
                    ph_qm = tph("qkvm", f_qm, after=mmdep)
                    slot_user[sl] = ph_qm
                    def f_qe(eng, ctx, g=g, half=half, pc=pc):
                        last = None
                        for ml in range(2):
                            c = g * 4 + half * 2 + ml
                            last = eng.tensor_scalar(out=V("qbR", c * T, T),
                                                     in0=PS[:, pc + ml * 256: pc + ml * 256 + T],
                                                     scalar1=bqkv_sb[:, c:c + 1], scalar2=None, op0=ALU.add)
                        return last
                    H["qe"][sub] = vph("qkve", f_qe, after={"tensor": ph_qm})
                    def f_qs(eng, ctx, g=g, half=half, pc=pc):
                        last = None
                        for ml in range(2):
                            c = g * 4 + half * 2 + ml
                            last = eng.activation(out=zsq[:, c * T:(c + 1) * T],
                                                  in_=PS[:, pc + ml * 256: pc + ml * 256 + T],
                                                  func=AF.Square, bias=bqkv_sb[:, c:c + 1], scale=1.0)
                        return last
                    H["qs"][sub] = sph("qkvs", f_qs, after={"tensor": ph_qm, "vector": H["qe"][sub]})

            def th_norm():
                def f_nm(eng, ctx):
                    last = None
                    for c in range(24):
                        last = eng.matmul(PS[0:24, 1024:1024 + T], ohR[:, c * 24:(c + 1) * 24],
                                          zsq[:, c * T:(c + 1) * T], start=(c == 0), stop=(c == 23))
                    return last
                H["nm"] = tph("norm", f_nm, after={"scalar": H["qs"][11], "vector": H["qe"][11]})
                H["rnc"] = vph("rnc", lambda eng, ctx: eng.tensor_single_scalar(
                    out=V("x_tm", 0, T)[0:24, :], in_=PS[0:24, 1024:1024 + T], scalar=1e-24, op=ALU.max),
                    after={"tensor": H["nm"], "sem": [(osem, 16 * no)]})
                ph_rl1 = sph("rnl1", lambda eng, ctx: eng.activation(out=V("x_tm", 512, T)[0:24, :],
                                                                     in_=V("x_tm", 0, T)[0:24, :], func=AF.Ln),
                             after={"vector": H["rnc"]})
                H["rl2"] = sph("rnl2", lambda eng, ctx: eng.activation(out=V("x_tm", 256, T)[0:24, :],
                                                                       in_=V("x_tm", 512, T)[0:24, :],
                                                                       func=AF.Exp, scale=-0.5),
                               after={"scalar": ph_rl1})

            def th_rnb():
                nonlocal nr
                dph("rnb1", lambda eng, ctx: ctx.dma(
                    s_rn[0:1, :].rearrange("o (c t) -> (o c) t", c=24), V("x_tm", 256, T)[0:24, :]), 1,
                    after={"scalar": H["rl2"]}, sem=rsem)
                nr += 1
                dph("rnb2", lambda eng, ctx: ctx.dma(
                    arF[0:1, regs["stB"][0]:regs["stB"][0] + 24 * T], s_rn[:]), 1,
                    after={"sem": [(rsem, 16 * nr)]}, sem=rsem)
                nr += 1

            def th_rb3():
                def f_rb3(eng, ctx):
                    eng.tensor_copy(out=zve[0:1, 0:4096],
                                    in_=arF[0:1, regs["stB"][0]:regs["stB"][0] + 4096])
                    return eng.tensor_copy(out=zsq[0:1, 4096:6144],
                                           in_=arF[0:1, regs["stB"][0] + 4096:regs["stB"][0] + 6144])
                rb3_dep = {"sem": [(rsem, 16 * nr)]}
                if deps.get("rb3_v") is not None:
                    rb3_dep["vector"] = deps["rb3_v"]
                H["rb3"] = vph("rnb3", f_rb3, after=rb3_dep)

            thunks = {"load": th_load, "xtr0": lambda: th_xtr(0), "xtr1": lambda: th_xtr(1),
                      "cast": th_cast, "qkv": th_qkv, "norm": th_norm, "rnb": th_rnb, "rb3": th_rb3}
            return thunks, H

        part1H = [None] * ntiles
        part1TH = [None] * ntiles
        part1deps = [None] * ntiles

        part1deps[0] = {"xl": None}
        part1TH[0], part1H[0] = make_part1(0, part1deps[0])
        part1TH[0]["load"]()
        part1TH[0]["xtr0"]()
        part1TH[0]["xtr1"]()
        part1TH[0]["cast"]()
        for g in range(6):
            part1TH[0]["qkv"](g)
        part1TH[0]["norm"]()
        part1TH[0]["rnb"]()
        part1TH[0]["rb3"]()

        for ti in range(ntiles):
            t0 = ti * T
            H1 = part1H[ti]
            if ti + 1 < ntiles:
                part1deps[ti + 1] = {}
                part1TH[ti + 1], part1H[ti + 1] = make_part1(ti + 1, part1deps[ti + 1])
                TH_n = part1TH[ti + 1]
            else:
                TH_n = None

            ph_ev = {}
            for w in range(6):
                pb = (w % 2) * 2048
                def f_em(eng, ctx, w=w, pb=pb):
                    last = None
                    for ml in range(4):
                        c = w * 4 + ml
                        last = eng.matmul(PS[:, pb + ml * 512:pb + ml * 512 + T], cosrow[:],
                                          rnrow(c), start=True, stop=True)
                        last = eng.matmul(PS[:, pb + ml * 512 + 256:pb + ml * 512 + 256 + T], sinrow[:],
                                          rnrow(c), start=True, stop=True)
                    return last
                enc0dep = {"vector": prev_oev if prev_oev is not None else H1["rb3"]}
                ph_em = tph("encm", f_em, after=enc0dep if w < 2 else {"vector": ph_ev[w - 2]})
                def f_ev(eng, ctx, w=w, pb=pb):
                    last = None
                    for ml in range(4):
                        c = w * 4 + ml
                        qbc = V("qbR", c * T, T)
                        mcos = PS[:, pb + ml * 512:pb + ml * 512 + T]
                        msin = PS[:, pb + ml * 512 + 256:pb + ml * 512 + 256 + T]
                        if c < 16:
                            reg = "stA" if c < 8 else "stAk"
                            h = c % 8
                            ore = V(reg, h * T, T)
                            oim = V(reg, 2048 + h * T, T)
                        else:
                            j = c - 16
                            ore = bass.AP(zve, j, [[16 * T, 128], [8, T]])
                            oim = bass.AP(zve, 8 * T + j, [[16 * T, 128], [8, T]])
                        eng.tensor_tensor(out=ore, in0=qbc, in1=mcos, op=ALU.mult)
                        last = eng.tensor_tensor(out=oim, in0=qbc, in1=msin, op=ALU.mult)
                    return last
                ph_ev[w] = vph("encv", f_ev, after={"tensor": ph_em})
            tapdump("stA", lambda: V("stA"), ti)
            tapdump("ve", lambda: f32(zve[:]), ti)

            T7 = 7 * T
            qsrc = ["stA", "stB", "stA"]; qdst = ["stB", "stA", "stB"]
            ksrc = ["stAk", "stBk", "stAk"]; kdst = ["stBk", "stAk", "stBk"]
            oQB = regs["qbR"][0]
            ph_blQ = ph_blK = ph_h7K = None
            for d in range(3):
                sqr, dqr = regs[qsrc[d]][0], regs[qdst[d]][0]
                skr, dkr = regs[ksrc[d]][0], regs[kdst[d]][0]
                qctl_w = (lambda: V("qbR", 0, T7)) if d == 0 else (lambda: zsq[:, 2 * T7:3 * T7])
                qctl = (lambda: V("qbR", 0, T7)) if d == 0 else (lambda: f32(zsq[:, 2 * T7:3 * T7]))
                kctl = (lambda: V("qbR", T7, T7)) if d == 0 else (lambda: V("qbR", 2 * T7, T7))
                def f_sqQ(eng, ctx, sqr=sqr):
                    return eng.activation(
                        out=zsq[:, 0:2 * T7].rearrange("p (h t) -> p h t", h=2),
                        in_=bass.AP(arF, sqr, [[AFC, 128], [2048, 2], [1, T7]]), func=AF.Square)
                dep = {"vector": ph_ev[3]} if d == 0 else {"vector": ph_blQ, "scalar": ph_h7K}
                ph_sqQ = sph("esqQ", f_sqQ, after=dep)
                ph_adQ = vph("eadQ", lambda eng, ctx, qctl_w=qctl_w: eng.tensor_tensor(
                    out=qctl_w(), in0=f32(zsq[:, 0:T7]), in1=f32(zsq[:, T7:2 * T7]),
                    op=ALU.add), after={"scalar": ph_sqQ})
                if d == 0:
                    def f_sqK(eng, ctx, skr=skr):
                        return eng.activation(
                            out=zsq[:, 0:2 * T7].rearrange("p (h t) -> p h t", h=2),
                            in_=bass.AP(arF, skr, [[AFC, 128], [2048, 2], [1, T7]]), func=AF.Square)
                    ph_sqK = sph("esqK", f_sqK, after={"scalar": ph_sqQ, "vector": ph_adQ})
                    ph_adK = vph("eadK", lambda eng, ctx, kctl=kctl: eng.tensor_tensor(
                        out=kctl(), in0=f32(zsq[:, 0:T7]), in1=f32(zsq[:, T7:2 * T7]),
                        op=ALU.add), after={"scalar": ph_sqK})
                else:
                    def f_sqK(eng, ctx, skr=skr):
                        return eng.activation(
                            out=bass.AP(arF, oQB, [[AFC, 128], [T7, 2], [1, T7]]),
                            in_=bass.AP(arF, skr, [[AFC, 128], [2048, 2], [1, T7]]), func=AF.Square)
                    ph_sqK = sph("esqK", f_sqK, after={"vector": ph_blK, "scalar": ph_sqQ})
                    ph_adK = vph("eadK", lambda eng, ctx, kctl=kctl: eng.tensor_tensor(
                        out=kctl(), in0=V("qbR", 0, T7), in1=V("qbR", T7, T7),
                        op=ALU.add), after={"scalar": ph_sqK})
                def f_pqQ(eng, ctx, sqr=sqr, qctl=qctl):
                    eng.tensor_tensor(out=HV("P", 0, T7), in0=qctl(),
                                      in1=bass.AP(arF, sqr + T, [[AFC, 128], [1, T7]]), op=ALU.mult)
                    return eng.tensor_tensor(out=HV("Q", 0, T7), in0=qctl(),
                                             in1=bass.AP(arF, sqr + 2048 + T, [[AFC, 128], [1, T7]]),
                                             op=ALU.mult)
                ph_pqQ = vph("epqQ", f_pqQ, after={"vector": ph_adQ})
                def f_mm(eng, ctx, d=d, pn="P", qn="Q"):
                    last = None
                    for p in range(7):
                        j = d * 7 + p
                        dA = diag[:, j * 128:j * 128 + 128]
                        dB = diag[:, (21 + j) * 128:(21 + j) * 128 + 128]
                        dC = diag[:, (42 + j) * 128:(42 + j) * 128 + 128]
                        Pp = HV(pn, p * T, T)
                        Qp = HV(qn, p * T, T)
                        eng.matmul(PS[:, p * T:(p + 1) * T], dA, Pp, start=True, stop=False)
                        eng.matmul(PS[:, p * T:(p + 1) * T], dB, Qp, start=False, stop=True)
                        eng.matmul(PS[:, 2048 + p * T:2048 + (p + 1) * T], dB, Pp, start=True, stop=False)
                        last = eng.matmul(PS[:, 2048 + p * T:2048 + (p + 1) * T], dC, Qp, start=False, stop=True)
                    return last
                ph_mmQ = tph("emmQ", f_mm, after={"vector": ph_pqQ})
                def f_pqK(eng, ctx, skr=skr, kctl=kctl):
                    eng.tensor_tensor(out=HV("P2", 0, T7), in0=kctl(),
                                      in1=bass.AP(arF, skr + T, [[AFC, 128], [1, T7]]), op=ALU.mult)
                    return eng.tensor_tensor(out=HV("Q2", 0, T7), in0=kctl(),
                                             in1=bass.AP(arF, skr + 2048 + T, [[AFC, 128], [1, T7]]),
                                             op=ALU.mult)
                ph_pqK = vph("epqK", f_pqK, after={"vector": ph_adK})
                def f_blend(eng, ctx, s=sqr, dd=dqr):
                    eng.scalar_tensor_tensor(out=bass.AP(arF, dd, [[AFC, 128], [1, T7]]),
                                             in0=PS[:, 0:T7], scalar=1.0,
                                             in1=bass.AP(arF, s + T, [[AFC, 128], [1, T7]]),
                                             op0=ALU.mult, op1=ALU.add)
                    return eng.scalar_tensor_tensor(out=bass.AP(arF, dd + 2048, [[AFC, 128], [1, T7]]),
                                                    in0=PS[:, 2048:2048 + T7], scalar=1.0,
                                                    in1=bass.AP(arF, s + 2048 + T, [[AFC, 128], [1, T7]]),
                                                    op0=ALU.mult, op1=ALU.add)
                ph_blQ = vph("eblQ", f_blend, after={"tensor": ph_mmQ})
                ph_mmK = tph("emmK", lambda eng, ctx, d=d: f_mm(eng, ctx, d=d, pn="P2", qn="Q2"),
                             after={"vector": ph_blQ})
                def f_blendK(eng, ctx, s=skr, dd=dkr):
                    return f_blend(eng, ctx, s=s, dd=dd)
                ph_blK = vph("eblK", f_blendK, after={"tensor": ph_mmK})
                if d == 0:
                    def f_h7Q(eng, ctx, s=sqr, dd=dqr):
                        return eng.copy(out=bass.AP(arF, dd + T7, [[AFC, 128], [2048, 2], [1, T]]),
                                        in_=bass.AP(arF, s + T7, [[AFC, 128], [2048, 2], [1, T]]))
                    ph_h7Q = sph("eh7Q", f_h7Q, after={"scalar": ph_sqK})
                    def f_h7K(eng, ctx, s=skr, dd=dkr):
                        return eng.copy(out=bass.AP(arF, dd + T7, [[AFC, 128], [2048, 2], [1, T]]),
                                        in_=bass.AP(arF, s + T7, [[AFC, 128], [2048, 2], [1, T]]))
                    ph_h7K = sph("eh7K", f_h7K, after={"scalar": ph_h7Q})
            ph_entend = ph_blK
            oA, oB = regs["stA"][0], regs["stB"][0]
            oAk, oBk = regs["stAk"][0], regs["stBk"][0]
            def f_trQ(eng, ctx):
                eng.copy(out=bass.AP(arF, oA, [[AFC, 128], [1, 8], [16, T]]),
                         in_=bass.AP(arF, oB, [[AFC, 128], [T, 8], [1, T]]))
                return eng.copy(out=bass.AP(arF, oA + 8, [[AFC, 128], [1, 8], [16, T]]),
                                in_=bass.AP(arF, oB + 2048, [[AFC, 128], [T, 8], [1, T]]))
            ph_trQ = sph("etrQ", f_trQ, after={"scalar": ph_h7K, "vector": ph_blQ})
            def f_trK(eng, ctx):
                eng.tensor_copy(out=bass.AP(arF, oAk, [[AFC, 128], [1, 8], [16, T]]),
                                in_=bass.AP(arF, oBk, [[AFC, 128], [T, 8], [1, T]]))
                return eng.tensor_copy(out=bass.AP(arF, oAk + 8, [[AFC, 128], [1, 8], [16, T]]),
                                       in_=bass.AP(arF, oBk + 2048, [[AFC, 128], [T, 8], [1, T]]))
            ph_trK = vph("etrK", f_trK, after={"vector": ph_blK, "scalar": ph_h7K})
            ph_kr1 = vph("ekr1", lambda eng, ctx: eng.tensor_copy(
                out=bass.AP(arF, oB, [[AFC, 128], [1, 8], [16, T]]),
                in_=bass.AP(arF, oBk + 2048, [[AFC, 128], [T, 8], [1, T]])),
                after={"vector": ph_trK, "scalar": ph_trQ})
            ph_kr2 = sph("ekr2", lambda eng, ctx: eng.activation(
                out=bass.AP(arF, oB + 8, [[AFC, 128], [1, 8], [16, T]]),
                in_=bass.AP(arF, oBk, [[AFC, 128], [T, 8], [1, T]]),
                func=AF.Copy, scale=-1.0),
                after={"scalar": ph_trQ, "vector": ph_blK})
            tapdump("stB", lambda: V("stA"), ti)

            def f_pat(eng, ctx):
                last = None
                for t in range(T):
                    last = eng.matmul(PS[0:8, t * 16: t * 16 + 16],
                                      V("stA", t * 16, 8), V("stAk", t * 16, 16),
                                      start=(t % 32 == 0), stop=False)
                    last = eng.matmul(PS[0:8, t * 16: t * 16 + 16],
                                      V("stA", t * 16 + 8, 8), V("stB", t * 16, 16),
                                      start=False, stop=(t % 32 == 31))
                return last
            ph_pat = tph("patm", f_pat, after={"scalar": ph_kr2, "vector": ph_kr1})
            Rre = lambda: PS[0:8, 0:16 * T].rearrange("p (t c j) -> p t c j", c=2, j=8)[:, :, 0, :]
            Rim = lambda: PS[0:8, 0:16 * T].rearrange("p (t c j) -> p t c j", c=2, j=8)[:, :, 1, :]

            S0v = lambda: V("stB", 0, T8)[0:8, :]
            EEv = lambda: V("stB", T8, T8)[0:32, :]
            PRv = lambda: V("stBk", 0, T8)[0:32, :]
            Wv = lambda: zsq[0:32, T8:2 * T8]
            M0v = lambda: V("qbR", 1024 + 2 * T8, T)[0:8, :]
            RSv = lambda: V("qbR", 1024 + 2 * T8 + T, T)[0:32, :]
            RS2v = lambda: V("qbR", 1024 + 2 * T8 + 2 * T, T)[0:32, :]
            def f_s0(eng, ctx):
                eng.activation(out=S0v(), in_=Rre(), func=AF.Square)
                return eng.activation(out=EEv()[0:8, :], in_=Rim(), func=AF.Square)
            ph_s0 = sph("smS0", f_s0, after={"tensor": ph_pat})
            ph_d1 = vph("smD1", lambda eng, ctx: eng.tensor_tensor(
                out=PRv()[0:8, :], in0=S0v(), in1=EEv()[0:8, :], op=ALU.add),
                after={"scalar": ph_s0})
            ph_d2 = vph("smD2", lambda eng, ctx: eng.tensor_reduce(
                out=M0v(), in_=PRv()[0:8, :].rearrange("p (t j) -> p t j", j=8), axis=AX.X, op=ALU.max),
                after={"vector": ph_d1})
            ph_d3 = vph("smD3", lambda eng, ctx: eng.tensor_tensor(
                out=zsq[0:8, T8:2 * T8].rearrange("p (t j) -> p t j", j=8),
                in0=PRv()[0:8, :].rearrange("p (t j) -> p t j", j=8),
                in1=M0v().unsqueeze(2).broadcast_to([8, T, 8]), op=ALU.subtract),
                after={"vector": ph_d2})
            Dv = lambda: zsq[0:8, T8:2 * T8]
            def f_ds(eng, ctx):
                last = None
                for q4 in range(T8 // 512):
                    last = eng.matmul(PS[0:32, q4 * 512:q4 * 512 + 512], e4R[:],
                                      Dv()[0:8, q4 * 512:(q4 + 1) * 512], start=True, stop=True)
                return last
            ph_ds = tph("smDS", f_ds, after={"vector": ph_d3, "scalar": ph_s0})
            ph_e = sph("smE", lambda eng, ctx: eng.activation(out=EEv(), in_=PS[0:32, 0:T8], func=AF.Exp),
                       after={"tensor": ph_ds})
            ph_p1 = vph("smP1", lambda eng, ctx: eng.tensor_reduce(
                out=RSv(), in_=EEv().rearrange("p (t j) -> p t j", j=8), axis=AX.X, op=ALU.add),
                after={"scalar": ph_e})
            ph_p2a = sph("smP2a", lambda eng, ctx: eng.activation(out=RS2v(), in_=RSv(), func=AF.Ln),
                         after={"vector": ph_p1})
            ph_p2 = sph("smP2b", lambda eng, ctx: eng.activation(out=RSv(), in_=RS2v(),
                                                                 func=AF.Exp, scale=-1.0),
                        after={"scalar": ph_p2a})
            ph_p3 = vph("smP3", lambda eng, ctx: eng.tensor_tensor(
                out=PRv().rearrange("p (t j) -> p t j", j=8),
                in0=EEv().rearrange("p (t j) -> p t j", j=8),
                in1=RSv().unsqueeze(2).broadcast_to([32, T, 8]), op=ALU.mult),
                after={"scalar": ph_p2})
            ph_w2 = sph("smW", lambda eng, ctx: eng.activation(out=Wv(), in_=PRv(), func=AF.Sqrt,
                                                               bias=kbias[0:32, 0:1]),
                        after={"vector": ph_p3})
            tapdump("W", lambda: f32(Wv()), ti)

            vsre = lambda: V("aux", 0, T)
            vsim = lambda: V("aux", T, T)
            cohre = lambda: V("x_tm", 640, T)
            cohim = lambda: V("x_tm", 896, T)
            cs = lambda i: V("qbR", i * 512, 512)
            def f_c1(eng, ctx):
                vev = f32(zve[:])
                eng.tensor_reduce(out=vsre(), in_=vev[:, 0:T8].rearrange("p (t j) -> p t j", j=8),
                                  axis=AX.X, op=ALU.add)
                return eng.tensor_reduce(out=vsim(), in_=vev[:, T8:2 * T8].rearrange("p (t j) -> p t j", j=8),
                                         axis=AX.X, op=ALU.add)
            ph_c1 = vph("coh1", f_c1, after={"vector": ph_entend})
            def f_c2(eng, ctx):
                eng.activation(out=cs(0)[:, 0:T], in_=vsre(), func=AF.Square)
                return eng.activation(out=cs(1)[:, 0:T], in_=vsim(), func=AF.Square)
            ph_c2 = sph("coh2", f_c2, after={"vector": ph_c1})
            ph_c3a = vph("coh3a", lambda eng, ctx: eng.tensor_tensor(
                out=cs(2)[:, 0:T], in0=cs(0)[:, 0:T], in1=cs(1)[:, 0:T], op=ALU.add),
                after={"scalar": ph_c2})
            ph_c3b = vph("coh3b", lambda eng, ctx: eng.tensor_single_scalar(
                out=cs(3)[:, 0:T], in_=cs(2)[:, 0:T], scalar=1e-36, op=ALU.max),
                after={"vector": ph_c3a})
            ph_c4a = sph("coh4a", lambda eng, ctx: eng.activation(out=cs(0)[:, 0:T], in_=cs(3)[:, 0:T], func=AF.Ln),
                         after={"vector": ph_c3b})
            ph_c4b = sph("coh4b", lambda eng, ctx: eng.activation(out=cs(1)[:, 0:T], in_=cs(0)[:, 0:T],
                                                                  func=AF.Exp, scale=-0.5),
                         after={"scalar": ph_c4a})
            def f_c5(eng, ctx):
                eng.tensor_tensor(out=cohre(), in0=vsre(), in1=cs(1)[:, 0:T], op=ALU.mult)
                return eng.tensor_tensor(out=cohim(), in0=vsim(), in1=cs(1)[:, 0:T], op=ALU.mult)
            ph_c5 = vph("coh5", f_c5, after={"scalar": ph_c4b, "sem": [(osem, 16 * no)]})

            msFre = lambda: V("stA", 0, T8)
            msFim = lambda: V("stA", T8, T8)
            FA = (lambda: zsq[:, 5120:5632], lambda: zsq[:, 5632:6144])
            FB = (lambda: zsq[:, 0:512], lambda: zsq[:, 512:1024])
            Frev, Fimv = FB
            vqTv = lambda: zsq[:, 4096:5120]
            wT4v = lambda: V("x_tm", 512, 128)
            statRv = lambda: zsq[:, 0:2048]
            ph_vqte_prev = None
            ph_fev = None
            ph_clq_last = None
            ph_g3_prev = None
            prep = {}

            def emit_prep(r):
                def f_vqt(eng, ctx, r=r):
                    last = None
                    for gl in range(4):
                        g = r * 4 + gl
                        for par in range(2):
                            last = eng.transpose(PS[:, (gl * 2 + par) * 128:(gl * 2 + par) * 128 + 128].bitcast(F32R),
                                                 zve[:, par * T8 + g * 128: par * T8 + (g + 1) * 128], eyeR[:])
                    return last
                vqt_dep = ({"scalar": ph_e, "vector": ph_c1} if r == 0
                           else {"scalar": ph_clq_last, "vector": ph_g3_prev})
                prep["vqt"] = tph("vqt", f_vqt, after=vqt_dep)
                def f_vqe2(eng, ctx):
                    last = None
                    for gl in range(4):
                        last = eng.copy(out=zsq[:, 4096 + gl * 256:4096 + gl * 256 + 128],
                                        in_=PS[:, (gl * 2) * 128:(gl * 2) * 128 + 128].bitcast(F32R))
                        last = eng.copy(out=zsq[:, 4096 + gl * 256 + 128:4096 + gl * 256 + 256],
                                        in_=PS[:, (gl * 2 + 1) * 128:(gl * 2 + 1) * 128 + 128].bitcast(F32R))
                    return last
                prep["vqte"] = sph("vqte", f_vqe2, after={"tensor": prep["vqt"], "scalar": ph_w2})
                def f_wtt(eng, ctx, r=r):
                    last = None
                    for gl in range(4):
                        last = eng.transpose(PS[:, 2048 + gl * 32: 2048 + gl * 32 + 32].bitcast(F32R),
                                             Wv()[0:32, (r * 4 + gl) * 128:(r * 4 + gl) * 128 + 128],
                                             eyeR[0:32, 0:32])
                    return last
                prep["wtt"] = tph("wtt", f_wtt, after={"scalar": prep["vqte"]})
                prep["wte"] = vph("wtte", lambda eng, ctx: eng.tensor_copy(out=wT4v(), in_=PS[:, 2048:2048 + 128]),
                                  after={"tensor": prep["wtt"], "vector": ph_c5})

            def emit_statb(extra):
                def f_stb(eng, ctx):
                    owT = regs["x_tm"][0] + 512
                    last = None
                    for c in range(4):
                        i0 = bass.AP(arF, owT + c * 8, [[AFC, 128], [32, 4], [0, 16], [1, 8]])
                        i1 = bass.AP(bmask, 0, [[128, 128], [0, 4], [8, 16], [1, 8]])
                        o2 = bass.AP(zsq, c * 512, [[24 * T, 128], [128, 4], [8, 16], [1, 8]])
                        last = eng.tensor_tensor(out=o2, in0=i0, in1=i1, op=ALU.mult)
                    return last
                stb_dep = {"vector": prep["wte"]}
                stb_dep.update(extra)
                prep["statb"] = gph("statb", f_stb, after=stb_dep)

            emit_prep(0)
            emit_statb({})
            for r in range(NR):
                ph_vqte = prep["vqte"]
                ph_statb = prep["statb"]
                def f_wmm(eng, ctx, chalf=0):
                    last = None
                    for c in (chalf * 2, chalf * 2 + 1):
                        for gl in range(4):
                            bank_col = c * 1024 + gl * 256
                            last = eng.matmul(PS[:, bank_col:bank_col + 256],
                                              statRv()[:, (c * 4 + gl) * 128:(c * 4 + gl) * 128 + 128],
                                              vqTv()[:, gl * 256:(gl + 1) * 256],
                                              start=(gl % 2 == 0), stop=(gl % 2 == 1))
                    return last
                wmm_dep = {"gpsimd": ph_statb, "scalar": ph_vqte if ph_fev is None else ph_fev}
                ph_wmm_a = tph("wmmA", lambda eng, ctx: f_wmm(eng, ctx, 0), after=wmm_dep)
                ph_wmm = tph("wmmB", lambda eng, ctx: f_wmm(eng, ctx, 1), after=wmm_dep)
                ps4all = lambda: PS[:, 0:4096].rearrange("p (c g two d) -> p c g two d", c=4, two=2, d=128)
                ps3all = lambda: PS[:, 0:4096].rearrange("p (cg two d) -> p cg two d", two=2, d=128)
                Acl = lambda o, n: V("qbR", o, n)
                def f_sq2(eng, ctx, chalf=0):
                    eng.activation(out=Acl(chalf * 1024, 1024).rearrange("p (cg d) -> p cg d", d=128),
                                   in_=ps3all()[:, chalf * 8:chalf * 8 + 8, 0, :], func=AF.Square)
                    return eng.activation(out=Acl(2048 + chalf * 1024, 1024).rearrange("p (cg d) -> p cg d", d=128),
                                          in_=ps3all()[:, chalf * 8:chalf * 8 + 8, 1, :], func=AF.Square)
                ph_sq_a = sph("clq2a", lambda eng, ctx: f_sq2(eng, ctx, 0),
                              after={"tensor": ph_wmm_a, "gpsimd": ph_statb})
                ph_sqim = sph("clq2b", lambda eng, ctx: f_sq2(eng, ctx, 1),
                              after={"tensor": ph_wmm, "scalar": ph_sq_a})
                ph_add = vph("cladd", lambda eng, ctx: eng.tensor_tensor(
                    out=Acl(4096, 2048), in0=Acl(0, 2048), in1=Acl(2048, 2048), op=ALU.add),
                    after={"scalar": ph_sqim})
                ph_ln = sph("clln", lambda eng, ctx: eng.activation(
                    out=Acl(0, 2048), in_=Acl(4096, 2048), func=AF.Ln, bias=kbias[:, 2:3]),
                    after={"vector": ph_add})
                ph_rsq = sph("clex", lambda eng, ctx: eng.activation(
                    out=Acl(2048, 2048), in_=Acl(0, 2048), func=AF.Exp, scale=-0.5),
                    after={"scalar": ph_ln})
                ph_g1 = vph("clg1", lambda eng, ctx: eng.tensor_scalar(
                    out=Acl(4096, 2048), in0=Acl(2048, 2048), scalar1=1e-6, scalar2=1.0,
                    op0=ALU.mult, op1=ALU.max), after={"scalar": ph_rsq})
                ph_g2 = vph("clg2", lambda eng, ctx: eng.tensor_tensor(
                    out=Acl(0, 2048), in0=Acl(2048, 2048), in1=Acl(4096, 2048), op=ALU.min),
                    after={"vector": ph_g1})
                g3dst = [(FB[0](), FB[1]()), (zsq[:, 5120:5632], zsq[:, 5632:6144]),
                         (Acl(2048, 512), Acl(2560, 512)), (Acl(3072, 512), Acl(3584, 512))]
                def f_g3(eng, ctx):
                    last = None
                    for c in range(4):
                        fc = Acl(c * 512, 512).rearrange("p (g d) -> p g d", d=128)
                        for par in range(2):
                            last = eng.scalar_tensor_tensor(
                                out=g3dst[c][par].rearrange("p (g d) -> p g d", d=128),
                                in0=fc, scalar=supw_sb[:, c:c + 1],
                                in1=ps4all()[:, c, :, par, :], op0=ALU.mult, op1=ALU.mult)
                    return last
                ph_g3 = vph("clg3", f_g3, after={"vector": ph_g2})
                ph_clq_last = ph_sqim
                ph_g3_prev = ph_g3
                if r + 1 < NR:
                    emit_prep(r + 1)
                def f_acc1(eng, ctx):
                    eng.tensor_tensor(out=Acl(4096, 1024), in0=f32(zsq[:, 0:1024]),
                                      in1=f32(zsq[:, 5120:6144]), op=ALU.add)
                    return eng.tensor_tensor(out=Acl(5120, 1024), in0=Acl(2048, 1024),
                                             in1=Acl(3072, 1024), op=ALU.add)
                ph_acc1 = vph("clac1", f_acc1, after={"vector": ph_g3})
                ph_acc2 = vph("clac2", lambda eng, ctx: eng.tensor_tensor(
                    out=zsq[:, 0:1024], in0=Acl(4096, 1024), in1=Acl(5120, 1024), op=ALU.add),
                    after={"vector": ph_acc1})
                prev_acc = ph_acc2
                ph_clq_last2v = prev_acc
                def f_ftr(eng, ctx):
                    last = None
                    for par, bufv in ((0, Frev), (1, Fimv)):
                        for gl in range(4):
                            last = eng.transpose(
                                PS[:, 1024 + par * 512 + gl * 128: 1024 + par * 512 + (gl + 1) * 128].bitcast(F32R),
                                bufv()[:, gl * 128:(gl + 1) * 128], eyeR[:])
                    return last
                ph_ftr = tph("ftr", f_ftr, after={"vector": prev_acc, "scalar": ph_clq_last})
                if r + 1 < NR:
                    emit_statb({"tensor": ph_ftr})
                def f_fev(eng, ctx, r=r):
                    oRe = regs["stA"][0]
                    oIm = regs["stA"][0] + T8
                    last = None
                    for par, oo in ((0, oRe), (1, oIm)):
                        o = bass.AP(arF, oo + r * 64, [[AFC, 128], [16, 4], [1, 16], [T, 8]])
                        last = eng.copy(out=o, in_=PS[:, 1024 + par * 512:1024 + (par + 1) * 512]
                                        .rearrange("p (g t i) -> p g t i", t=16, i=8))
                    return last
                ph_fev = sph("fev", f_fev, after={"tensor": ph_ftr})
                ph_vqte_prev = ph_vqte
            tapdump("msF", msFre, ti)

            def f_meas1(eng, ctx):
                chb_re = cohre().unsqueeze(1).broadcast_to([128, 8, T])
                chb_im = cohim().unsqueeze(1).broadcast_to([128, 8, T])
                eng.tensor_tensor(out=zsq[:, T8:2 * T8].rearrange("p (i t) -> p i t", i=8),
                                  in0=msFre().rearrange("p (i t) -> p i t", i=8), in1=chb_re, op=ALU.mult)
                return eng.tensor_tensor(out=zsq[:, 2 * T8:3 * T8].rearrange("p (i t) -> p i t", i=8),
                                         in0=msFim().rearrange("p (i t) -> p i t", i=8), in1=chb_im, op=ALU.mult)
            ph_m1 = vph("measv1", f_meas1, after={"scalar": ph_fev, "vector": ph_c5})
            ph_m2 = vph("measv2", lambda eng, ctx: eng.tensor_tensor(
                out=HV("meas", 0, T8), in0=f32(zsq[:, T8:2 * T8]), in1=f32(zsq[:, 2 * T8:3 * T8]),
                op=ALU.subtract), after={"vector": ph_m1})
            tapdump("meas", lambda: f32(zsq[:, T8:2 * T8]), ti)

            if TH_n is not None:
                part1deps[ti + 1]["xl"] = {"vector": ph_clq_last2v}
                part1deps[ti + 1]["xtr_sc"] = ph_fev
                part1deps[ti + 1]["xtr_v"] = ph_clq_last2v
                TH_n["load"]()

            ph_woe = {}
            ph_wom = {}
            for g in range(2):
                sl = nfetch % 2
                nfetch += 1
                def f_wow(eng, ctx, g=g, sl=sl):
                    return ctx.dma(wslot[:, sl * 4096:sl * 4096 + 4096], s_wout[:, g * 4096:(g + 1) * 4096])
                ph_wow = dph("woutw", f_wow, 1, after={"tensor": slot_user[sl]}, sem=wsem[sl])
                nwv[sl] += 1
                def f_wom(eng, ctx, g=g, sl=sl):
                    last = None
                    pc = 2048 + (g % 2) * 1024
                    for ml in range(4):
                        for k in range(8):
                            last = eng.matmul(PS[:, pc + ml * 256:pc + ml * 256 + T],
                                              wslot[:, sl * 4096 + (ml * 8 + k) * 128:sl * 4096 + (ml * 8 + k) * 128 + 128],
                                              HV("meas", k * T, T), start=(k == 0), stop=(k == 7))
                    return last
                womdep = {"sem": [(wsem[sl], 16 * nwv[sl])], "vector": ph_m2, "scalar": ph_fev}
                ph_wom[g] = tph("woutm", f_wom, after=womdep)
                slot_user[sl] = ph_wom[g]
                def f_woe(eng, ctx, g=g):
                    last = None
                    pc = 2048 + (g % 2) * 1024
                    for ml in range(4):
                        c = g * 4 + ml
                        last = eng.scalar_tensor_tensor(out=zve[:, c * T:(c + 1) * T],
                                                        in0=PS[:, pc + ml * 256:pc + ml * 256 + T],
                                                        scalar=bout_sb[:, c:c + 1],
                                                        in1=f32(zxf[:, c * T:(c + 1) * T]),
                                                        op0=ALU.add, op1=ALU.add)
                    return last
                ph_woe[g] = vph("woute", f_woe, after={"tensor": ph_wom[g]})
                if TH_n is not None:
                    part1deps[ti + 1][f"xe{g}_v"] = ph_woe[g]
                    if g == 0:
                        TH_n["xtr0"]()
                    else:
                        TH_n["xtr1"]()
                        TH_n["cast"]()
            x1v = lambda: zve[:, 0:T8]
            x2v = lambda: zve[:, T8:2 * T8]
            tapdump("x1", lambda: f32(x1v()), ti)

            def layernorm(srcv, dstv, bf_name, dep0):
                ph_lsq = sph("lnsq", lambda eng, ctx: eng.activation(out=zsq[:, 0:T8], in_=f32(srcv()),
                                                                     func=AF.Square), after=dep0)
                def f_lmm(eng, ctx):
                    last = None
                    for k in range(8):
                        last = eng.matmul(PS[0:1, 3584:3584 + T], onescR[:],
                                          srcv()[:, k * T:(k + 1) * T], start=(k == 0), stop=(k == 7))
                    for k in range(8):
                        last = eng.matmul(PS[0:1, 3584 + 256:3584 + 256 + T], onescR[:],
                                          zsq[:, k * T:(k + 1) * T], start=(k == 0), stop=(k == 7))
                    return last
                ph_lmm = tph("lnmm", f_lmm, after={"scalar": ph_lsq})
                ph_l1 = vph("lnth1", lambda eng, ctx: (
                    eng.tensor_single_scalar(out=thin[0:1, 0:T], in_=PS[0:1, 3584:3584 + T],
                                             scalar=1.0 / 1024.0, op=ALU.mult),
                    eng.tensor_single_scalar(out=thin[0:1, 256:256 + T],
                                             in_=PS[0:1, 3584 + 256:3584 + 256 + T],
                                             scalar=1.0 / 1024.0, op=ALU.mult))[-1],
                    after={"tensor": ph_lmm})
                ph_l2 = vph("lnth2", lambda eng, ctx: eng.tensor_tensor(
                    out=thin[0:1, 512:512 + T], in0=thin[0:1, 0:T],
                    in1=thin[0:1, 0:T], op=ALU.mult), after={"vector": ph_l1})
                ph_l3 = vph("lnth3", lambda eng, ctx: eng.tensor_tensor(
                    out=thin[0:1, 768:768 + T], in0=thin[0:1, 256:256 + T],
                    in1=thin[0:1, 512:512 + T], op=ALU.subtract), after={"vector": ph_l2})
                ph_r1 = sph("lnrs1", lambda eng, ctx: eng.activation(
                    out=thin[0:1, 1024:1024 + T], in_=thin[0:1, 768:768 + T],
                    func=AF.Ln, bias=kbias[0:1, 1:2]), after={"vector": ph_l3})
                ph_r2 = sph("lnrs2", lambda eng, ctx: eng.activation(
                    out=thin[0:1, 512:512 + T], in_=thin[0:1, 1024:1024 + T], func=AF.Exp, scale=-0.5),
                    after={"scalar": ph_r1})
                ph_m1x = vph("lnmr1", lambda eng, ctx: (
                    eng.tensor_copy(out=zsq[0:1, 0:T], in_=thin[0:1, 512:512 + T]),
                    eng.scalar_tensor_tensor(out=thin[0:1, 768:768 + T], in0=thin[0:1, 0:T],
                                             scalar=-1.0, in1=thin[0:1, 512:512 + T],
                                             op0=ALU.mult, op1=ALU.mult))[-1],
                    after={"scalar": ph_r2})
                ph_m2x = vph("lnmr2", lambda eng, ctx: eng.tensor_copy(
                    out=zsq[0:1, 256:256 + T], in_=thin[0:1, 768:768 + T]),
                    after={"vector": ph_m1x})
                def f_lbc(eng, ctx):
                    eng.matmul(PS[:, 3072:3072 + T], onesrR[:], zsq[0:1, 0:T], start=True, stop=True)
                    return eng.matmul(PS[:, 3584:3584 + T], onesrR[:], zsq[0:1, 256:256 + T],
                                      start=True, stop=True)
                ph_lbc = tph("lnbc", f_lbc, after={"vector": ph_m2x})
                ph_a1 = vph("lnap1", lambda eng, ctx: eng.tensor_tensor(
                    out=zsq[:, 0:T8].rearrange("p (k t) -> p k t", k=8),
                    in0=f32(srcv()).rearrange("p (k t) -> p k t", k=8),
                    in1=PS[:, 3072:3072 + T].unsqueeze(1).broadcast_to([128, 8, T]), op=ALU.mult),
                    after={"tensor": ph_lbc})
                ph_a2 = vph("lnap2", lambda eng, ctx: eng.tensor_tensor(
                    out=dstv().rearrange("p (k t) -> p k t", k=8),
                    in0=f32(zsq[:, 0:T8]).rearrange("p (k t) -> p k t", k=8),
                    in1=PS[:, 3584:3584 + T].unsqueeze(1).broadcast_to([128, 8, T]), op=ALU.add),
                    after={"vector": ph_a1})
                ph_lc = None
                if bf_name:
                    ph_lc = sph("lncast", lambda eng, ctx: eng.copy(out=HV(bf_name, 0, T8), in_=f32(dstv())),
                                after={"vector": ph_a2})
                return ph_a2, ph_lc

            ph_ln1, ph_lc1 = layernorm(x1v, x2v, "x2bf", {"vector": ph_woe[1]})
            tapdump("x2", lambda: f32(x2v()), ti)

            ph_fsl = None
            ph_fh = None
            ph_fdm = None
            for hg in range(3):
              for gw in (hg * 2, hg * 2 + 1):
                sl = nfetch % 2
                nfetch += 1
                def f_fgw(eng, ctx, gw=gw, sl=sl):
                    return ctx.dma(wslot[:, sl * 4096:sl * 4096 + 4096], s_wg[:, gw * 4096:(gw + 1) * 4096])
                ph_fgw = dph("ffgw", f_fgw, 1, after={"tensor": slot_user[sl]}, sem=wsem[sl])
                nwv[sl] += 1
                fgm_dep = {"sem": [(wsem[sl], 16 * nwv[sl])], "scalar": ph_lc1 if ph_fsl is None else ph_fsl}
                if ph_fh is not None:
                    fgm_dep["vector"] = ph_fh
                def f_fgm(eng, ctx, sl=sl):
                    last = None
                    for ml in range(4):
                        for k in range(8):
                            last = eng.matmul(PS[:, ml * 256:ml * 256 + T],
                                              wslot[:, sl * 4096 + (ml * 8 + k) * 128:sl * 4096 + (ml * 8 + k) * 128 + 128],
                                              HV("x2bf", k * T, T), start=(k == 0), stop=(k == 7))
                    return last
                ph_fgm = tph("ffgm", f_fgm, after=fgm_dep)
                slot_user[sl] = ph_fgm
                def f_fsl(eng, ctx, gw=gw):
                    last = None
                    for ml in range(4):
                        last = eng.activation(out=HV("meas", (gw % 2) * 4 * T + ml * T, T),
                                              in_=PS[:, ml * 256:ml * 256 + T], func=AF.Silu, scale=INV127)
                    return last
                ph_fsl = sph("ffsl", f_fsl, after={"tensor": ph_fgm})
                sl = nfetch % 2
                nfetch += 1
                def f_fuw(eng, ctx, gw=gw, sl=sl):
                    return ctx.dma(wslot[:, sl * 4096:sl * 4096 + 4096], s_wu[:, gw * 4096:(gw + 1) * 4096])
                ph_fuw = dph("ffuw", f_fuw, 1, after={"tensor": slot_user[sl]}, sem=wsem[sl])
                nwv[sl] += 1
                fum_sw = [(wsem[sl], 16 * nwv[sl])]
                def f_fum(eng, ctx, sl=sl):
                    last = None
                    for ml in range(4):
                        for k in range(8):
                            last = eng.matmul(PS[:, ml * 256:ml * 256 + T],
                                              wslot[:, sl * 4096 + (ml * 8 + k) * 128:sl * 4096 + (ml * 8 + k) * 128 + 128],
                                              HV("x2bf", k * T, T), start=(k == 0), stop=(k == 7))
                    return last
                ph_fum = tph("ffum", f_fum, after={"sem": fum_sw, "scalar": ph_fsl})
                slot_user[sl] = ph_fum
                def f_fh(eng, ctx, gw=gw):
                    last = None
                    for ml in range(4):
                        last = eng.scalar_tensor_tensor(out=HV("hbuf", (gw % 2) * 4 * T + ml * T, T),
                                                        in0=PS[:, ml * 256:ml * 256 + T], scalar=INV127,
                                                        in1=HV("meas", (gw % 2) * 4 * T + ml * T, T),
                                                        op0=ALU.mult, op1=ALU.mult)
                    return last
                ph_fh = vph("ffh", f_fh, after={"tensor": ph_fum, "scalar": ph_fsl})
                sl = nfetch % 2
                nfetch += 1
                def f_fdw(eng, ctx, gw=gw, sl=sl):
                    return ctx.dma(wslot[:, sl * 4096:sl * 4096 + 4096], s_wd[:, gw * 4096:(gw + 1) * 4096])
                ph_fdw = dph("ffdw", f_fdw, 1, after={"tensor": slot_user[sl]}, sem=wsem[sl])
                nwv[sl] += 1
                fdm_sw = [(wsem[sl], 16 * nwv[sl])]
                def f_fdm(eng, ctx, gw=gw, sl=sl):
                    last = None
                    for ml in range(8):
                        for kl in range(4):
                            last = eng.matmul(PS[:, 2048 + ml * 256: 2048 + ml * 256 + T],
                                              wslot[:, sl * 4096 + (kl * 8 + ml) * 128:sl * 4096 + (kl * 8 + ml) * 128 + 128],
                                              HV("hbuf", (gw % 2) * 4 * T + kl * T, T),
                                              start=(gw == 0 and kl == 0 and ml % 2 == 0),
                                              stop=(gw == 5 and kl == 3 and ml % 2 == 1))
                    return last
                ph_fdm = tph("ffdm", f_fdm, after={"sem": fdm_sw, "vector": ph_fh})
                slot_user[sl] = ph_fdm
                if TH_n is not None:
                    TH_n["qkv"](gw)
            if TH_n is not None:
                TH_n["norm"]()
                TH_n["rnb"]()
            def f_fde(eng, ctx):
                last = None
                for ml in range(8):
                    last = eng.scalar_tensor_tensor(out=zsq[:, 2048 + ml * T:2048 + (ml + 1) * T],
                                                    in0=PS[:, 2048 + ml * 256:2048 + ml * 256 + T], scalar=INV127,
                                                    in1=f32(x2v()[:, ml * T:(ml + 1) * T]), op0=ALU.mult, op1=ALU.add)
                return last
            fde_dep = {"tensor": ph_fdm}
            if TH_n is not None:
                fde_dep["tensor"] = part1H[ti + 1]["nm"]
            ph_fde = vph("ffde", f_fde, after=fde_dep)
            x3rawv = lambda: zsq[:, 2048:4096]
            tapdump("x3raw", lambda: f32(zsq[:, 2048:4096]), ti)
            if TH_n is not None:
                part1deps[ti + 1]["rb3_v"] = ph_fde
                TH_n["rb3"]()

            x3v = lambda: V("stBk", 0, T8)
            ln2dep = {"vector": ph_fde}
            if TH_n is not None:
                ln2dep["tensor"] = part1H[ti + 1]["nm"]
            ph_ln2, _ = layernorm(x3rawv, x3v, None, ln2dep)
            tapdump("x3", x3v, ti)

            def f_otr(eng, ctx):
                last = None
                for s in range(nsub):
                    for k in range(8):
                        last = eng.transpose(PS[:, (s * 8 + k) * 128:(s * 8 + k) * 128 + 128],
                                             x3v()[:, k * T + s * 128: k * T + (s + 1) * 128], eyeF[:])
                return last
            otr_sc = part1H[ti + 1]["qs"][11] if TH_n is not None else ph_fsl
            prev_otr = tph("otr", f_otr, after={"vector": ph_ln2, "scalar": otr_sc})
            oev_dep = {"tensor": prev_otr, "sem": [(osem, 16 * no), (rsem, 16 * nr)]}
            prev_oev = vph("oev", lambda eng, ctx: eng.tensor_copy(out=V("x_tm"), in_=PS[:, 0:T8]),
                           after=oev_dep)
            def f_ost(eng, ctx, t0=t0):
                last = None
                for s in range(nsub):
                    last = ctx.dma(y_out[t0 + s * 128: t0 + (s + 1) * 128, :], V("x_tm", s * 1024, 1024))
                return last
            prev_ost = dph("ost", f_ost, nsub, after={"vector": prev_oev}, sem=osem)
            no += nsub

        ph = sch.phase("fin", after={"all": True, "sem": [(osem, 16 * no)]})
        def f_fin(eng, ctx):
            return eng.engine_nop()
        sch.on(ph, "vector", f_fin)

        with nc.Block() as block:
            totE, totD = sch.emit(block, esems, dsem)
        nins = sum(len(b.instructions) if hasattr(b, 'instructions') else 0 for b in [])
        print(f"[sched] phases={len(sch.phases)} totE={totE} totD={totD}")

    return nc


def kernel_inputs(inp, tokens, ncores=8):
    consts = host_consts()
    common = {
        "wqkv": np.ascontiguousarray(np.asarray(inp["L0_attn_W_qkv"], np.float32)),
        "bqkv": np.ascontiguousarray(np.asarray(inp["L0_attn_b_qkv"], np.float32)).reshape(24, 128),
        "wout": np.ascontiguousarray(np.asarray(inp["L0_attn_W_out"], np.float32)),
        "bout": np.ascontiguousarray(np.asarray(inp["L0_attn_b_out"], np.float32)).reshape(8, 128),
        "qwrow": np.ascontiguousarray(np.asarray(inp["L0_attn_quantum_weights"], np.float32)).reshape(1, 256),
        "pp": np.ascontiguousarray(np.asarray(inp["L0_attn_photonic_phases"], np.float32)),
        "gaterow": np.ascontiguousarray(np.asarray(inp["L0_attn_entanglement_gates"], np.float32)).reshape(1, 96),
        "logrow": np.ascontiguousarray(np.asarray(inp["L0_attn_superposition_logits"], np.float32)).reshape(1, 4),
        "wg": np.ascontiguousarray(np.asarray(inp["L0_ffn_w_gate"], np.int32)),
        "wu": np.ascontiguousarray(np.asarray(inp["L0_ffn_w_up"], np.int32)),
        "wd": np.ascontiguousarray(np.asarray(inp["L0_ffn_w_down"], np.int32)),
    }
    for k, v in consts.items():
        common[k] = v
    x = np.ascontiguousarray(np.asarray(inp["L0_attn_query"], dtype=np.float32))
    B, S, D = x.shape
    in_maps = []
    for i in range(ncores):
        m = dict(common)
        xi = x[i % B].reshape(S, D)[:tokens]
        m["x"] = np.ascontiguousarray(xi)
        in_maps.append(m)
    return in_maps


def run_dev(inp, tokens=4096, T=256, ncores=8, tap=None):
    nc = build_nc(tokens, T, tap=tap)
    in_maps = kernel_inputs(inp, tokens, ncores)
    res = run_bass_kernel_spmd(nc, in_maps, core_ids=list(range(ncores)))
    outs = np.stack([res.results[i]["y"] for i in range(ncores)], axis=0).astype(np.float32)
    dbgs = [res.results[i].get("dbg") for i in range(ncores)]
    return outs, dbgs


_NC_CACHE = {}


def kernel(**inputs):
    tokens, T, ncores = 4096, 256, 8
    key = (tokens, T)
    if key not in _NC_CACHE:
        _NC_CACHE[key] = build_nc(tokens, T)
    nc = _NC_CACHE[key]
    in_maps = kernel_inputs(inputs, tokens, ncores)
    res = run_bass_kernel_spmd(nc, in_maps, core_ids=list(range(ncores)))
    out = np.stack([res.results[i]["y"] for i in range(ncores)], axis=0).astype(np.float32)
    return out


# version 27 (final)
# speedup vs baseline: 1.0015x; 1.0015x over previous
import os
import numpy as np
import ml_dtypes
import concourse.bass as bass
import concourse.mybir as mybir
from concourse.bass_utils import run_bass_kernel_spmd

F32 = mybir.dt.float32
F32R = mybir.dt.float32r
BF16 = mybir.dt.bfloat16
I32 = mybir.dt.int32
AF = mybir.ActivationFunctionType
ALU = mybir.AluOpType
AX = mybir.AxisListType

E = 1024
SCALING = float(128 ** (-0.5))
INV127 = float(np.float32(1.0) / np.float32(127.0))
PI = float(np.pi)
TWO_PI = float(2 * np.pi)
INV2PI = float(1.0 / (2 * np.pi))

ENGS = ("sync", "tensor", "vector", "scalar", "gpsimd")


class Sched:
    def __init__(self):
        self.phases = []

    def phase(self, name, after=None):
        ph = {"name": name, "work": {e: [] for e in ENGS}, "after": after,
              "idx": len(self.phases)}
        self.phases.append(ph)
        return ph

    def on(self, ph, eng, fn, dmas=0, compute=None, dmasem=None):
        if compute is None:
            compute = (dmas == 0)
        ph["work"][eng].append((fn, dmas, compute, dmasem))

    def emit(self, block, esems, dsem):
        totE = {e: [] for e in ENGS}
        totD = []
        for ph in self.phases:
            for e in ENGS:
                totE[e].append(sum(1 for (_, _, c, _) in ph["work"][e] if c))
            totD.append(sum(nd for e in ENGS for (_, nd, _, ds) in ph["work"][e] if ds is None))
        cumE = {e: np.cumsum([0] + totE[e]) for e in ENGS}
        cumD = np.cumsum([0] + totD)

        def make_section(eng_name):
            def section(eng):
                class Ctx:
                    pass
                ctx = Ctx()
                ctx.dma = lambda out, in_: eng.dma_start(out=out, in_=in_).then_inc(dsem, 16)
                lastE = {e: 0 for e in ENGS}
                lastD = 0
                for k, ph in enumerate(self.phases):
                    lst = ph["work"][eng_name]
                    if not lst:
                        continue
                    after = ph["after"]
                    semwaits = []
                    if after is None or after.get("all"):
                        reqs = [(e, int(cumE[e][k])) for e in ENGS]
                        reqD = int(cumD[k])
                        if after is not None:
                            semwaits = after.get("sem", [])
                    else:
                        reqs = []
                        reqD = 0
                        for e, dep_ph in after.items():
                            if e == "dma":
                                reqD = int(cumD[dep_ph["idx"] + 1])
                            elif e == "sem":
                                semwaits = dep_ph
                            else:
                                reqs.append((e, int(cumE[e][dep_ph["idx"] + 1])))
                    for e, needE in reqs:
                        if needE > lastE[e]:
                            eng.wait_ge(esems[e], needE)
                            lastE[e] = needE
                    if reqD > lastD:
                        eng.wait_ge(dsem, 16 * reqD)
                        lastD = reqD
                    for (s, v) in semwaits:
                        eng.wait_ge(s, v)
                    for (fn, nd, comp, ds) in lst:
                        if ds is not None:
                            ctx.dma = lambda out, in_, ds=ds: eng.dma_start(out=out, in_=in_).then_inc(ds, 16)
                        else:
                            ctx.dma = lambda out, in_: eng.dma_start(out=out, in_=in_).then_inc(dsem, 16)
                        last = fn(eng, ctx)
                        if comp:
                            assert last is not None, f"phase {ph['name']} on {eng_name} returned None"
                            last.then_inc(esems[eng_name], 1)
            return section

        for name in ENGS:
            getattr(block, name)(make_section(name))
        return sum(int(cumE[e][-1]) for e in ENGS), int(cumD[-1])


def host_consts():
    c = {}
    c["c_eye_f32"] = np.eye(128, dtype=np.float32)
    oh = np.zeros((24, 128, 24), np.float32)
    for cc in range(24):
        oh[cc, :, cc] = 1.0
    c["c_oh"] = oh.reshape(24 * 128, 24)
    e4 = np.zeros((8, 4, 8), np.float32)
    for i in range(8):
        e4[i, :, i] = 1.0
    c["c_e4pat"] = e4.reshape(8, 32)
    bm = np.zeros((128, 16, 8), np.float32)
    for t in range(16):
        bm[t * 8:(t + 1) * 8, t, :] = 1.0
    c["c_blockmask"] = bm.reshape(128, 128)
    s16 = np.zeros((16, 128), np.float32)
    for p in range(128):
        s16[p % 16, p] = 1.0
    c["c_s16"] = s16
    return c


def build_nc(tokens, T, tap=None):
    ntiles = tokens // T
    nsub = T // 128
    NG = T // 16
    NR = NG // 4
    assert tokens % T == 0 and T % 128 == 0 and NG % 4 == 0

    nc = bass.Bass("TRN2", target_bir_lowering=False)

    x_in = nc.dram_tensor("x", [tokens, E], F32, kind="ExternalInput").ap()
    wqkv = nc.dram_tensor("wqkv", [3072, E], F32, kind="ExternalInput").ap()
    bqkv = nc.dram_tensor("bqkv", [24, 128], F32, kind="ExternalInput").ap()
    wout = nc.dram_tensor("wout", [E, E], F32, kind="ExternalInput").ap()
    bout = nc.dram_tensor("bout", [8, 128], F32, kind="ExternalInput").ap()
    qwrow = nc.dram_tensor("qwrow", [1, 256], F32, kind="ExternalInput").ap()
    pp = nc.dram_tensor("pp", [16, E], F32, kind="ExternalInput").ap()
    gaterow = nc.dram_tensor("gaterow", [1, 96], F32, kind="ExternalInput").ap()
    logrow = nc.dram_tensor("logrow", [1, 4], F32, kind="ExternalInput").ap()
    wg = nc.dram_tensor("wg", [3072, E], I32, kind="ExternalInput").ap()
    wu = nc.dram_tensor("wu", [3072, E], I32, kind="ExternalInput").ap()
    wd = nc.dram_tensor("wd", [E, 3072], I32, kind="ExternalInput").ap()
    consts_np = host_consts()
    cin = {}
    for k, v in consts_np.items():
        dt = BF16 if v.dtype == ml_dtypes.bfloat16 else F32
        cin[k] = nc.dram_tensor(k, list(v.shape), dt, kind="ExternalInput").ap()

    y_out = nc.dram_tensor("y", [tokens, E], F32, kind="ExternalOutput").ap()
    dbg = nc.dram_tensor("dbg", [128, 4096 if tap else 4], F32, kind="ExternalOutput").ap()

    s_wqkv = nc.dram_tensor("s_wqkv", [128, 24 * 8 * 128], BF16).ap()
    s_wout = nc.dram_tensor("s_wout", [128, 8 * 8 * 128], BF16).ap()
    s_wg = nc.dram_tensor("s_wg", [128, 24 * 8 * 128], BF16).ap()
    s_wu = nc.dram_tensor("s_wu", [128, 24 * 8 * 128], BF16).ap()
    s_wd = nc.dram_tensor("s_wd", [128, 24 * 8 * 128], BF16).ap()
    s_rn = nc.dram_tensor("s_rn", [1, 24 * T], F32).ap()

    sch = Sched()
    import contextlib
    with contextlib.ExitStack() as stack:
        T8 = 8 * T
        T16 = 16 * T
        regs = {}
        off = 0
        def R(name, cols):
            nonlocal off
            regs[name] = (off, cols)
            off += cols
        R("x_tm", T8)
        R("qbR", 24 * T)
        R("stA", T16)
        R("stAk", T16)
        R("stB", T16)
        R("stBk", T16)
        R("aux", 2 * T)
        regs["krhs"] = regs["stA"]
        AFC = off
        arF = stack.enter_context(nc.sbuf_tensor("arF", [128, AFC], F32))
        zsq = stack.enter_context(nc.sbuf_tensor("zsq", [128, 24 * T], F32R))
        zve = stack.enter_context(nc.sbuf_tensor("zve", [128, 16 * T], F32R))
        zxf = stack.enter_context(nc.sbuf_tensor("zxf", [128, 8 * T], F32R))

        hregs = {"x_bf": (0, T8), "meas": (T8, T8), "x2bf": (2 * T8, T8), "hbuf": (3 * T8, T8)}
        hregs["P"] = (0, 7 * T)
        hregs["Q"] = (T8, 7 * T)
        hregs["P2"] = (2 * T8, 7 * T)
        hregs["Q2"] = (3 * T8, 7 * T)
        AHC = 4 * T8
        arH = stack.enter_context(nc.sbuf_tensor("arH", [128, AHC], BF16))

        wslot = stack.enter_context(nc.sbuf_tensor("wslot", [128, 8192], BF16))
        diag = stack.enter_context(nc.sbuf_tensor("diag", [128, 63 * 128], BF16))
        eyeF = stack.enter_context(nc.sbuf_tensor("eyeF", [128, 128], F32))
        ohR = stack.enter_context(nc.sbuf_tensor("ohR", [128, 24 * 24], F32R))
        eyeR = stack.enter_context(nc.sbuf_tensor("eyeR", [128, 128], F32R))
        onescR = stack.enter_context(nc.sbuf_tensor("onescR", [128, 1], F32R))
        onesrR = stack.enter_context(nc.sbuf_tensor("onesrR", [1, 128], F32R))
        cosrow = stack.enter_context(nc.sbuf_tensor("cosrow", [1, 128], F32R))
        sinrow = stack.enter_context(nc.sbuf_tensor("sinrow", [1, 128], F32R))
        e4R = stack.enter_context(nc.sbuf_tensor("e4R", [8, 32], F32R))
        bmask = stack.enter_context(nc.sbuf_tensor("bmask", [128, 128], F32))
        bqkv_sb = stack.enter_context(nc.sbuf_tensor("bqkv_sb", [128, 24], F32))
        bout_sb = stack.enter_context(nc.sbuf_tensor("bout_sb", [128, 8], F32))
        supw_sb = stack.enter_context(nc.sbuf_tensor("supw_sb", [128, 4], F32))
        kbias = stack.enter_context(nc.sbuf_tensor("kbias", [128, 3], F32))
        thin = stack.enter_context(nc.sbuf_tensor("thin", [1, 1792], F32))
        PS = stack.enter_context(nc.psum_tensor([128, 4096], F32))

        esems = {e: stack.enter_context(nc.semaphore(f"esem_{e}")) for e in ENGS}
        dsem = stack.enter_context(nc.semaphore("dsem"))
        wsem = [stack.enter_context(nc.semaphore("wsemA")), stack.enter_context(nc.semaphore("wsemB"))]
        xsem = stack.enter_context(nc.semaphore("xsem"))
        osem = stack.enter_context(nc.semaphore("osem"))
        rsem = stack.enter_context(nc.semaphore("rsem"))

        print(f"[sbuf] arF={AFC*4//1024}KB arH={AHC*2//1024}KB wslot=16KB diag=16KB "
              f"misc~{(128*4*4 + 576*4 + 2048*4 + 512*4 + 24*512*4)//1024 + 2}KB")

        def V(name, o2=0, n=None):
            o, cols = regs[name]
            if n is None:
                n = cols - o2
            return arF[:, o + o2: o + o2 + n]

        def HV(name, o2=0, n=None):
            o, cols = hregs[name]
            if n is None:
                n = cols - o2
            return arH[:, o + o2: o + o2 + n]

        def rawF(name, o2, dims):
            o, _ = regs[name]
            return bass.AP(arF, o + o2, [[AFC, 128]] + dims)

        def rawH(name, o2, dims):
            o, _ = hregs[name]
            return bass.AP(arH, o + o2, [[AHC, 128]] + dims)

        def r32(apx):
            return apx.bitcast(F32R)

        def f32(apx):
            return apx.bitcast(F32)

        ph = sch.phase("cl")
        def f_cl(eng, ctx):
            ctx.dma(eyeF[:], cin["c_eye_f32"][:])
            ctx.dma(bmask[:], cin["c_blockmask"][:])
            for cc in range(24):
                ctx.dma(V("qbR", cc * 24, 24), cin["c_oh"][cc * 128:(cc + 1) * 128, :])
            ctx.dma(V("qbR", 1024, 128)[0:16, :], cin["c_s16"][:])
            ctx.dma(V("qbR", 1152, 32)[0:8, :], cin["c_e4pat"][:])
            ctx.dma(V("qbR", 1184, 128)[0:16, :], pp[:, 0:128])
            ctx.dma(thin[0:1, 0:96], gaterow[:])
            ctx.dma(thin[0:1, 96:352], qwrow[:])
            ctx.dma(thin[0:1, 352:356], logrow[:])
            ctx.dma(V("qbR", 1312, 128)[0:24, :], bqkv[:])
            return ctx.dma(V("qbR", 1440, 128)[0:8, :], bout[:])
        sch.on(ph, "sync", f_cl, dmas=34, compute=False)

        ph = sch.phase("cm0")
        def f_cm0(eng, ctx):
            eng.memset(V("qbR", 576, 1), 1.0)
            eng.memset(kbias[:, 0:1], 1e-12)
            eng.memset(kbias[:, 1:2], 1e-5)
            eng.memset(kbias[:, 2:3], 2e-26)
            eng.memset(V("qbR", 577, 128)[0:1, :], 1.0)
            return eng.memset(thin[0:1, 558:560], 0.0)
        sch.on(ph, "vector", f_cm0)
        ph = sch.phase("cr")
        def f_cr(eng, ctx):
            eng.tensor_copy(out=ohR[:], in_=V("qbR", 0, 576))
            eng.tensor_copy(out=eyeR[:], in_=eyeF[:])
            eng.tensor_copy(out=onescR[:], in_=V("qbR", 576, 1))
            eng.tensor_copy(out=zsq[0:1, 191:192], in_=kbias[0:1, 0:1])
            return eng.tensor_copy(out=onesrR[:], in_=V("qbR", 577, 128)[0:1, :])
        sch.on(ph, "vector", f_cr)

        ph = sch.phase("prep")
        def f_rep(eng, ctx):
            return eng.matmul(PS[:, 0:128], V("qbR", 1024, 128)[0:16, :], V("qbR", 1184, 128)[0:16, :],
                              start=True, stop=True)
        sch.on(ph, "tensor", f_rep)
        ph = sch.phase("pmask")
        def f_pm(eng, ctx):
            return eng.tensor_tensor(out=zsq[:, 0:128], in0=PS[:, 0:128], in1=eyeF[:], op=ALU.mult)
        sch.on(ph, "vector", f_pm)
        ph = sch.phase("phirow")
        def f_pr(eng, ctx):
            return eng.matmul(PS[0:1, 512:640], onescR[:], zsq[:, 0:128], start=True, stop=True)
        sch.on(ph, "tensor", f_pr)

        def vphase(name, fn):
            ph2 = sch.phase(name)
            sch.on(ph2, "vector", fn)
        g3r = lambda: thin[0:1, 0:96].rearrange("p (d g f) -> p d g f", d=3, f=4)
        vphase("ang1", lambda eng, ctx: (
            eng.tensor_copy(out=thin[0:1, 384:512], in_=PS[0:1, 512:640]),
            eng.tensor_single_scalar(out=thin[0:1, 512:533].rearrange("p (d g) -> p d g", d=3),
                                     in_=g3r()[:, :, 0:7, 0], scalar=0.5, op=ALU.mult),
            eng.tensor_copy(out=thin[0:1, 533:554].rearrange("p (d g) -> p d g", d=3),
                            in_=g3r()[:, :, 0:7, 1]),
            eng.tensor_copy(out=thin[0:1, 554:558],
                            in_=thin[0:1, 96:352].rearrange("p (c r) -> p c r", c=4)[:, :, 0]))[-1])
        for it in range(4):
            xo = 384 if it % 2 == 0 else 736
            xn = 736 if it % 2 == 0 else 384
            def f_w1(eng, ctx, xo=xo):
                eng.tensor_single_scalar(out=thin[0:1, 560:736], in_=thin[0:1, xo:xo + 176],
                                         scalar=PI, op=ALU.is_gt)
                return eng.tensor_single_scalar(out=thin[0:1, 1088:1264], in_=thin[0:1, xo:xo + 176],
                                                scalar=-PI, op=ALU.is_lt)
            vphase("angw1", f_w1)
            vphase("angw2", lambda eng, ctx: eng.tensor_tensor(
                out=thin[0:1, 912:1088], in0=thin[0:1, 560:736], in1=thin[0:1, 1088:1264], op=ALU.subtract))
            def f_w3(eng, ctx, xo=xo, xn=xn):
                return eng.scalar_tensor_tensor(out=thin[0:1, xn:xn + 176], in0=thin[0:1, 912:1088],
                                                scalar=-TWO_PI, in1=thin[0:1, xo:xo + 176],
                                                op0=ALU.mult, op1=ALU.add)
            vphase("angw3", f_w3)
        vphase("angc1", lambda eng, ctx: eng.tensor_single_scalar(
            out=thin[0:1, 560:736], in_=thin[0:1, 384:560], scalar=PI / 2, op=ALU.add))
        vphase("angc2", lambda eng, ctx: eng.tensor_single_scalar(
            out=thin[0:1, 1088:1264], in_=thin[0:1, 560:736], scalar=PI, op=ALU.is_gt))
        vphase("angc3", lambda eng, ctx: eng.scalar_tensor_tensor(
            out=thin[0:1, 912:1088], in0=thin[0:1, 1088:1264], scalar=-TWO_PI,
            in1=thin[0:1, 560:736], op0=ALU.mult, op1=ALU.add))
        ph = sch.phase("trig")
        def f_trig(eng, ctx):
            eng.activation(out=thin[0:1, 1264:1440], in_=thin[0:1, 384:560], func=AF.Sin)
            return eng.activation(out=thin[0:1, 1440:1616], in_=thin[0:1, 912:1088], func=AF.Sin)
        sch.on(ph, "scalar", f_trig)

        SV, CV = 1264, 1440
        ct = lambda: thin[0:1, CV + 128:CV + 149]
        st = lambda: thin[0:1, SV + 128:SV + 149]
        cp = lambda: thin[0:1, CV + 149:CV + 170]
        sp = lambda: thin[0:1, SV + 149:SV + 170]
        vphase("coef1", lambda eng, ctx: (
            eng.tensor_copy(out=cosrow[:], in_=thin[0:1, CV:CV + 128]),
            eng.tensor_copy(out=sinrow[:], in_=thin[0:1, SV:SV + 128]),
            eng.tensor_tensor(out=thin[0:1, 1679:1700], in0=st(), in1=cp(), op=ALU.mult),
            eng.tensor_tensor(out=thin[0:1, 1637:1658], in0=st(), in1=sp(), op=ALU.mult),
            eng.tensor_scalar(out=thin[0:1, 1700:1704], in0=thin[0:1, SV + 170:SV + 174],
                              scalar1=0.3, scalar2=0.5, op0=ALU.mult, op1=ALU.add),
            eng.tensor_reduce(out=thin[0:1, 1712:1713], in_=thin[0:1, 352:356], axis=AX.X, op=ALU.max))[-1])
        vphase("coef2", lambda eng, ctx: (
            eng.tensor_tensor(out=thin[0:1, 1616:1637], in0=ct(), in1=thin[0:1, 1679:1700], op=ALU.add),
            eng.tensor_tensor(out=thin[0:1, 1658:1679], in0=ct(), in1=thin[0:1, 1679:1700], op=ALU.subtract),
            eng.tensor_scalar(out=thin[0:1, 1704:1708], in0=thin[0:1, 1700:1704], scalar1=-1.0, scalar2=1.0,
                              op0=ALU.mult, op1=ALU.add),
            eng.tensor_tensor(out=thin[0:1, 1713:1717], in0=thin[0:1, 352:356],
                              in1=thin[0:1, 1712:1713].broadcast_to([1, 4]), op=ALU.subtract))[-1])
        vphase("coef3", lambda eng, ctx: (
            eng.tensor_single_scalar(out=thin[0:1, 1616:1637], in_=thin[0:1, 1616:1637], scalar=-1.0, op=ALU.add),
            eng.tensor_single_scalar(out=thin[0:1, 1658:1679], in_=thin[0:1, 1658:1679], scalar=-1.0, op=ALU.add),
            eng.tensor_tensor(out=thin[0:1, 1708:1712], in0=thin[0:1, 1700:1704],
                              in1=thin[0:1, 1704:1708], op=ALU.mult))[-1])
        ph = sch.phase("supe")
        def f_supe(eng, ctx):
            return eng.activation(out=thin[0:1, 1717:1721], in_=thin[0:1, 1713:1717], func=AF.Exp)
        sch.on(ph, "scalar", f_supe)
        vphase("supn1", lambda eng, ctx: (
            eng.tensor_single_scalar(out=thin[0:1, 1708:1712], in_=thin[0:1, 1708:1712],
                                     scalar=SCALING, op=ALU.mult),
            eng.tensor_reduce(out=thin[0:1, 1721:1722], in_=thin[0:1, 1717:1721], axis=AX.X, op=ALU.add))[-1])
        vphase("supn2", lambda eng, ctx: eng.reciprocal(out=thin[0:1, 1722:1723], in_=thin[0:1, 1721:1722]))
        vphase("supn3", lambda eng, ctx: eng.tensor_tensor(
            out=thin[0:1, 1723:1727], in0=thin[0:1, 1717:1721],
            in1=thin[0:1, 1722:1723].broadcast_to([1, 4]), op=ALU.mult))
        vphase("supn4", lambda eng, ctx: (
            eng.tensor_copy(out=zsq[0:1, 128:149], in_=thin[0:1, 1616:1637]),
            eng.tensor_copy(out=zsq[0:1, 149:170], in_=thin[0:1, 1637:1658]),
            eng.tensor_copy(out=zsq[0:1, 170:191], in_=thin[0:1, 1658:1679]),
            eng.tensor_copy(out=zsq[0:1, 192:196], in_=thin[0:1, 1708:1712]),
            eng.tensor_copy(out=zsq[0:1, 196:200], in_=thin[0:1, 1723:1727]))[-1])

        ph = sch.phase("repl")
        def f_repl(eng, ctx):
            eng.matmul(PS[:, 0:64], onesrR[:], zsq[0:1, 128:192], start=True, stop=True)
            eng.matmul(PS[0:8, 512:516], onesrR[0:1, 0:8], zsq[0:1, 192:196], start=True, stop=True)
            eng.matmul(PS[:, 1024:1028], onesrR[:], zsq[0:1, 196:200], start=True, stop=True)
            eng.transpose(PS[:, 1536:1560], V("qbR", 1312, 128)[0:24, :], eyeF[0:24, 0:24])
            return eng.transpose(PS[:, 2048:2056], V("qbR", 1440, 128)[0:8, :], eyeF[0:8, 0:8])
        sch.on(ph, "tensor", f_repl)
        ph = sch.phase("repl2")
        def f_repl2(eng, ctx):
            eng.tensor_copy(out=V("qbR", 1700, 63), in_=PS[:, 0:63])
            eng.tensor_copy(out=supw_sb[:], in_=PS[:, 1024:1028])
            eng.tensor_copy(out=bqkv_sb[:], in_=PS[:, 1536:1560])
            eng.tensor_copy(out=bout_sb[:], in_=PS[:, 2048:2056])
            a8 = PS[0:8, 512:516].unsqueeze(2).broadcast_to([8, 4, 8])
            return eng.tensor_tensor(out=e4R[:].rearrange("p (c i) -> p c i", c=4),
                                     in0=V("qbR", 1152, 32)[0:8, :].rearrange("p (c i) -> p c i", c=4),
                                     in1=a8, op=ALU.mult)
        sch.on(ph, "vector", f_repl2)
        ph = sch.phase("diagb")
        def f_diag(eng, ctx):
            last = None
            for j in range(63):
                last = eng.tensor_scalar(out=diag[:, j * 128:(j + 1) * 128], in0=eyeF[:],
                                         scalar1=V("qbR", 1700 + j, 1), scalar2=None, op0=ALU.mult)
            return last
        sch.on(ph, "vector", f_diag)

        def wconv(srcT, scratch, nrb, nech, colfn, isint=False):
            for rb in range(nrb):
                ph1 = sch.phase("wv")
                def f_load(eng, ctx, rb=rb, srcT=srcT, nech=nech, isint=isint):
                    dst = V("stA", 0, nech * 128)
                    if isint:
                        dst = dst.bitcast(I32)
                    return ctx.dma(dst, srcT[rb * 128:(rb + 1) * 128, :])
                sch.on(ph1, "sync", f_load, dmas=1, compute=False)
                if isint:
                    ph2 = sch.phase("wv")
                    def f_cast(eng, ctx, nech=nech):
                        return eng.tensor_copy(out=V("stAk", 0, nech * 128),
                                               in_=V("stA", 0, nech * 128).bitcast(I32))
                    sch.on(ph2, "vector", f_cast)
                    srcreg = "stAk"
                else:
                    srcreg = "stA"
                ph3 = sch.phase("wv")
                def f_tr(eng, ctx, nech=nech, srcreg=srcreg):
                    last = None
                    for k in range(nech):
                        last = eng.transpose(PS[:, k * 128:(k + 1) * 128],
                                             V(srcreg, k * 128, 128), eyeF[:])
                    return last
                sch.on(ph3, "tensor", f_tr)
                ph4 = sch.phase("wv")
                def f_ev(eng, ctx, nech=nech):
                    return eng.copy(out=arH[:, 4096:4096 + nech * 128], in_=PS[:, 0:nech * 128])
                sch.on(ph4, "scalar", f_ev)
                ph5 = sch.phase("wv")
                def f_st(eng, ctx, rb=rb, nech=nech, scratch=scratch, colfn=colfn):
                    last = None
                    for k in range(nech):
                        last = ctx.dma(scratch[:, colfn(rb, k):colfn(rb, k) + 128],
                                       arH[:, 4096 + k * 128:4096 + (k + 1) * 128])
                    return last
                sch.on(ph5, "sync", f_st, dmas=nech, compute=False)

        wconv(wqkv, s_wqkv, 24, 8, lambda m, k: (m * 8 + k) * 128)
        wconv(wout, s_wout, 8, 8, lambda m, k: (m * 8 + k) * 128)
        wconv(wg, s_wg, 24, 8, lambda m, k: (m * 8 + k) * 128, isint=True)
        wconv(wu, s_wu, 24, 8, lambda m, k: (m * 8 + k) * 128, isint=True)
        wconv(wd, s_wd, 8, 24, lambda m, k: (k * 8 + m) * 128, isint=True)

        def vph(name, fn, after=None):
            p2 = sch.phase(name, after=after)
            sch.on(p2, "vector", fn)
            return p2

        def sph(name, fn, after=None):
            p2 = sch.phase(name, after=after)
            sch.on(p2, "scalar", fn)
            return p2

        def tph(name, fn, after=None):
            p2 = sch.phase(name, after=after)
            sch.on(p2, "tensor", fn)
            return p2

        def gph(name, fn, after=None):
            p2 = sch.phase(name, after=after)
            sch.on(p2, "gpsimd", fn)
            return p2

        def dph(name, fn, ndma, after=None, sem=None):
            p2 = sch.phase(name, after=after)
            sch.on(p2, "sync", fn, dmas=ndma, compute=False, dmasem=sem)
            return p2

        def tapdump(name, apfn, tile_i):
            if tap and tap[0] == tile_i and tap[1] == name:
                ph = sch.phase(f"tap_{name}")
                def f_tap(eng, ctx):
                    apx = apfn()
                    return ctx.dma(dbg[0:apx.partition_size(), 0:apx.free_size()], apx)
                sch.on(ph, "sync", f_tap, dmas=1, compute=False)
                sch.phase("tapb")

        last_pre = sch.phases[-1]
        slot_user = [None, None]
        nfetch = 0
        nwv = [0, 0]
        nx = 0
        no = 0
        nr = 0
        prev_ost = None
        prev_oev = None
        prev_otr = None

        rnrow = lambda c: (zve[0:1, c * T:(c + 1) * T] if c < 16
                           else zsq[0:1, 4096 + (c - 16) * T: 4096 + (c - 15) * T])

        def make_part1(ti, deps):
            t0 = ti * T
            H = {"qe": {}, "qs": {}}

            def th_load():
                nonlocal nx
                def f_xl(eng, ctx, t0=t0):
                    last = None
                    for s in range(nsub):
                        last = ctx.dma(V("qbR", 4096 + s * 1024, 1024),
                                       x_in[t0 + s * 128: t0 + (s + 1) * 128, :])
                    return last
                H["xl"] = dph("xload", f_xl, nsub, sem=xsem, after=deps.get("xl"))
                nx += nsub

            def th_xtr(w):
                def f_xt(eng, ctx, w=w):
                    last = None
                    for k in range(4):
                        for s in range(nsub):
                            idx = k * nsub + s
                            last = eng.transpose(PS[:, 1024 + idx * 128:1024 + idx * 128 + 128],
                                                 V("qbR", 4096 + s * 1024 + (w * 4 + k) * 128, 128),
                                                 eyeF[:])
                    return last
                xtdep = {"sem": [(xsem, 16 * nx)]}
                if w == 0:
                    if deps.get("xtr_sc") is not None:
                        xtdep["scalar"] = deps["xtr_sc"]
                    if deps.get("xtr_v") is not None:
                        xtdep["vector"] = deps["xtr_v"]
                else:
                    xtdep["vector"] = H["xe0"]
                H[f"xt{w}"] = tph("xtr", f_xt, after=xtdep)
                xe_dep = {"tensor": H[f"xt{w}"]}
                if deps.get(f"xe{w}_v") is not None:
                    xe_dep["vector"] = deps[f"xe{w}_v"]
                H[f"xe{w}"] = vph("xev", lambda eng, ctx, w=w: eng.tensor_copy(
                    out=zxf[:, w * 1024:(w + 1) * 1024], in_=PS[:, 1024:2048]),
                    after=xe_dep)

            def th_cast():
                H["xc"] = sph("xcast", lambda eng, ctx: eng.copy(out=HV("x_bf", 0, T8), in_=f32(zxf[:])),
                              after={"vector": H["xe1"]})

            def th_qkv(g):
                nonlocal nfetch
                sl = nfetch % 2
                nfetch += 1
                def f_ww(eng, ctx, g=g, sl=sl):
                    return ctx.dma(wslot[:, sl * 4096:sl * 4096 + 4096], s_wqkv[:, g * 4096:(g + 1) * 4096])
                wdep = None if slot_user[sl] is None else {"tensor": slot_user[sl]}
                dph("qkvw", f_ww, 1, after=wdep, sem=wsem[sl])
                nwv[sl] += 1
                for half in range(2):
                    sub = g * 2 + half
                    pc = 1024 + (sub % 2) * 512
                    mmdep = {"sem": [(wsem[sl], 16 * nwv[sl])],
                             "scalar": H["qs"].get(sub - 2, H["xc"]),
                             "vector": H["qe"].get(sub - 2, H["xe1"])}
                    def f_qm(eng, ctx, sl=sl, half=half, pc=pc):
                        last = None
                        for ml in range(2):
                            mo = half * 2 + ml
                            for k in range(8):
                                last = eng.matmul(PS[:, pc + ml * 256: pc + ml * 256 + T],
                                                  wslot[:, sl * 4096 + (mo * 8 + k) * 128: sl * 4096 + (mo * 8 + k) * 128 + 128],
                                                  HV("x_bf", k * T, T), start=(k == 0), stop=(k == 7))
                        return last
                    ph_qm = tph("qkvm", f_qm, after=mmdep)
                    slot_user[sl] = ph_qm
                    def f_qe(eng, ctx, g=g, half=half, pc=pc):
                        last = None
                        for ml in range(2):
                            c = g * 4 + half * 2 + ml
                            last = eng.tensor_scalar(out=V("qbR", c * T, T),
                                                     in0=PS[:, pc + ml * 256: pc + ml * 256 + T],
                                                     scalar1=bqkv_sb[:, c:c + 1], scalar2=None, op0=ALU.add)
                        return last
                    H["qe"][sub] = vph("qkve", f_qe, after={"tensor": ph_qm})
                    def f_qs(eng, ctx, g=g, half=half, pc=pc):
                        last = None
                        for ml in range(2):
                            c = g * 4 + half * 2 + ml
                            last = eng.activation(out=zsq[:, c * T:(c + 1) * T],
                                                  in_=PS[:, pc + ml * 256: pc + ml * 256 + T],
                                                  func=AF.Square, bias=bqkv_sb[:, c:c + 1], scale=1.0)
                        return last
                    H["qs"][sub] = sph("qkvs", f_qs, after={"tensor": ph_qm, "vector": H["qe"][sub]})

            def th_norm():
                def f_nm(eng, ctx):
                    last = None
                    for c in range(24):
                        last = eng.matmul(PS[0:24, 1024:1024 + T], ohR[:, c * 24:(c + 1) * 24],
                                          zsq[:, c * T:(c + 1) * T], start=(c == 0), stop=(c == 23))
                    return last
                H["nm"] = tph("norm", f_nm, after={"scalar": H["qs"][11], "vector": H["qe"][11]})
                H["rnc"] = vph("rnc", lambda eng, ctx: eng.tensor_single_scalar(
                    out=V("x_tm", 0, T)[0:24, :], in_=PS[0:24, 1024:1024 + T], scalar=1e-24, op=ALU.max),
                    after={"tensor": H["nm"], "sem": [(osem, 16 * no)]})
                ph_rl1 = sph("rnl1", lambda eng, ctx: eng.activation(out=V("x_tm", 512, T)[0:24, :],
                                                                     in_=V("x_tm", 0, T)[0:24, :], func=AF.Ln),
                             after={"vector": H["rnc"]})
                H["rl2"] = sph("rnl2", lambda eng, ctx: eng.activation(out=V("x_tm", 256, T)[0:24, :],
                                                                       in_=V("x_tm", 512, T)[0:24, :],
                                                                       func=AF.Exp, scale=-0.5),
                               after={"scalar": ph_rl1})

            def th_rnb():
                nonlocal nr
                dph("rnb1", lambda eng, ctx: ctx.dma(
                    s_rn[0:1, :].rearrange("o (c t) -> (o c) t", c=24), V("x_tm", 256, T)[0:24, :]), 1,
                    after={"scalar": H["rl2"]}, sem=rsem)
                nr += 1
                dph("rnb2", lambda eng, ctx: ctx.dma(
                    arF[0:1, regs["stB"][0]:regs["stB"][0] + 24 * T], s_rn[:]), 1,
                    after={"sem": [(rsem, 16 * nr)]}, sem=rsem)
                nr += 1

            def th_rb3():
                def f_rb3(eng, ctx):
                    eng.tensor_copy(out=zve[0:1, 0:4096],
                                    in_=arF[0:1, regs["stB"][0]:regs["stB"][0] + 4096])
                    return eng.tensor_copy(out=zsq[0:1, 4096:6144],
                                           in_=arF[0:1, regs["stB"][0] + 4096:regs["stB"][0] + 6144])
                rb3_dep = {"sem": [(rsem, 16 * nr)]}
                if deps.get("rb3_v") is not None:
                    rb3_dep["vector"] = deps["rb3_v"]
                H["rb3"] = vph("rnb3", f_rb3, after=rb3_dep)

            thunks = {"load": th_load, "xtr0": lambda: th_xtr(0), "xtr1": lambda: th_xtr(1),
                      "cast": th_cast, "qkv": th_qkv, "norm": th_norm, "rnb": th_rnb, "rb3": th_rb3}
            return thunks, H

        part1H = [None] * ntiles
        part1TH = [None] * ntiles
        part1deps = [None] * ntiles

        part1deps[0] = {"xl": None}
        part1TH[0], part1H[0] = make_part1(0, part1deps[0])
        part1TH[0]["load"]()
        part1TH[0]["xtr0"]()
        part1TH[0]["xtr1"]()
        part1TH[0]["cast"]()
        for g in range(6):
            part1TH[0]["qkv"](g)
        part1TH[0]["norm"]()
        part1TH[0]["rnb"]()
        part1TH[0]["rb3"]()

        for ti in range(ntiles):
            t0 = ti * T
            H1 = part1H[ti]
            if ti + 1 < ntiles:
                part1deps[ti + 1] = {}
                part1TH[ti + 1], part1H[ti + 1] = make_part1(ti + 1, part1deps[ti + 1])
                TH_n = part1TH[ti + 1]
            else:
                TH_n = None

            ph_ev = {}
            for w in range(6):
                pb = (w % 2) * 2048
                def f_em(eng, ctx, w=w, pb=pb):
                    last = None
                    for ml in range(4):
                        c = w * 4 + ml
                        last = eng.matmul(PS[:, pb + ml * 512:pb + ml * 512 + T], cosrow[:],
                                          rnrow(c), start=True, stop=True)
                        last = eng.matmul(PS[:, pb + ml * 512 + 256:pb + ml * 512 + 256 + T], sinrow[:],
                                          rnrow(c), start=True, stop=True)
                    return last
                enc0dep = {"vector": prev_oev if prev_oev is not None else H1["rb3"]}
                ph_em = tph("encm", f_em, after=enc0dep if w < 2 else {"vector": ph_ev[w - 2]})
                def f_ev(eng, ctx, w=w, pb=pb):
                    last = None
                    for ml in range(4):
                        c = w * 4 + ml
                        qbc = V("qbR", c * T, T)
                        mcos = PS[:, pb + ml * 512:pb + ml * 512 + T]
                        msin = PS[:, pb + ml * 512 + 256:pb + ml * 512 + 256 + T]
                        if c < 16:
                            reg = "stA" if c < 8 else "stAk"
                            h = c % 8
                            ore = V(reg, h * T, T)
                            oim = V(reg, 2048 + h * T, T)
                        else:
                            j = c - 16
                            ore = bass.AP(zve, j, [[16 * T, 128], [8, T]])
                            oim = bass.AP(zve, 8 * T + j, [[16 * T, 128], [8, T]])
                        eng.tensor_tensor(out=ore, in0=qbc, in1=mcos, op=ALU.mult)
                        last = eng.tensor_tensor(out=oim, in0=qbc, in1=msin, op=ALU.mult)
                    return last
                ph_ev[w] = vph("encv", f_ev, after={"tensor": ph_em})
            tapdump("stA", lambda: V("stA"), ti)
            tapdump("ve", lambda: f32(zve[:]), ti)

            T7 = 7 * T
            qsrc = ["stA", "stB", "stA"]; qdst = ["stB", "stA", "stB"]
            ksrc = ["stAk", "stBk", "stAk"]; kdst = ["stBk", "stAk", "stBk"]
            oQB = regs["qbR"][0]
            ph_blQ = ph_blK = ph_h7K = None
            for d in range(3):
                sqr, dqr = regs[qsrc[d]][0], regs[qdst[d]][0]
                skr, dkr = regs[ksrc[d]][0], regs[kdst[d]][0]
                qctl_w = (lambda: V("qbR", 0, T7)) if d == 0 else (lambda: zsq[:, 2 * T7:3 * T7])
                qctl = (lambda: V("qbR", 0, T7)) if d == 0 else (lambda: f32(zsq[:, 2 * T7:3 * T7]))
                kctl = (lambda: V("qbR", T7, T7)) if d == 0 else (lambda: V("qbR", 2 * T7, T7))
                def f_sqQ(eng, ctx, sqr=sqr):
                    return eng.activation(
                        out=zsq[:, 0:2 * T7].rearrange("p (h t) -> p h t", h=2),
                        in_=bass.AP(arF, sqr, [[AFC, 128], [2048, 2], [1, T7]]), func=AF.Square)
                dep = {"vector": ph_ev[3]} if d == 0 else {"vector": ph_blQ, "scalar": ph_h7K}
                ph_sqQ = sph("esqQ", f_sqQ, after=dep)
                ph_adQ = vph("eadQ", lambda eng, ctx, qctl_w=qctl_w: eng.tensor_tensor(
                    out=qctl_w(), in0=f32(zsq[:, 0:T7]), in1=f32(zsq[:, T7:2 * T7]),
                    op=ALU.add), after={"scalar": ph_sqQ})
                if d == 0:
                    def f_sqK(eng, ctx, skr=skr):
                        return eng.activation(
                            out=zsq[:, 0:2 * T7].rearrange("p (h t) -> p h t", h=2),
                            in_=bass.AP(arF, skr, [[AFC, 128], [2048, 2], [1, T7]]), func=AF.Square)
                    ph_sqK = sph("esqK", f_sqK, after={"scalar": ph_sqQ, "vector": ph_adQ})
                    ph_adK = vph("eadK", lambda eng, ctx, kctl=kctl: eng.tensor_tensor(
                        out=kctl(), in0=f32(zsq[:, 0:T7]), in1=f32(zsq[:, T7:2 * T7]),
                        op=ALU.add), after={"scalar": ph_sqK})
                else:
                    def f_sqK(eng, ctx, skr=skr):
                        return eng.activation(
                            out=bass.AP(arF, oQB, [[AFC, 128], [T7, 2], [1, T7]]),
                            in_=bass.AP(arF, skr, [[AFC, 128], [2048, 2], [1, T7]]), func=AF.Square)
                    ph_sqK = sph("esqK", f_sqK, after={"vector": ph_blK, "scalar": ph_sqQ})
                    ph_adK = vph("eadK", lambda eng, ctx, kctl=kctl: eng.tensor_tensor(
                        out=kctl(), in0=V("qbR", 0, T7), in1=V("qbR", T7, T7),
                        op=ALU.add), after={"scalar": ph_sqK})
                def f_pqQ(eng, ctx, sqr=sqr, qctl=qctl):
                    eng.tensor_tensor(out=HV("P", 0, T7), in0=qctl(),
                                      in1=bass.AP(arF, sqr + T, [[AFC, 128], [1, T7]]), op=ALU.mult)
                    return eng.tensor_tensor(out=HV("Q", 0, T7), in0=qctl(),
                                             in1=bass.AP(arF, sqr + 2048 + T, [[AFC, 128], [1, T7]]),
                                             op=ALU.mult)
                ph_pqQ = vph("epqQ", f_pqQ, after={"vector": ph_adQ})
                def f_mm(eng, ctx, d=d, pn="P", qn="Q"):
                    last = None
                    for p in range(7):
                        j = d * 7 + p
                        dA = diag[:, j * 128:j * 128 + 128]
                        dB = diag[:, (21 + j) * 128:(21 + j) * 128 + 128]
                        dC = diag[:, (42 + j) * 128:(42 + j) * 128 + 128]
                        Pp = HV(pn, p * T, T)
                        Qp = HV(qn, p * T, T)
                        eng.matmul(PS[:, p * T:(p + 1) * T], dA, Pp, start=True, stop=False)
                        eng.matmul(PS[:, p * T:(p + 1) * T], dB, Qp, start=False, stop=True)
                        eng.matmul(PS[:, 2048 + p * T:2048 + (p + 1) * T], dB, Pp, start=True, stop=False)
                        last = eng.matmul(PS[:, 2048 + p * T:2048 + (p + 1) * T], dC, Qp, start=False, stop=True)
                    return last
                ph_mmQ = tph("emmQ", f_mm, after={"vector": ph_pqQ})
                def f_pqK(eng, ctx, skr=skr, kctl=kctl):
                    eng.tensor_tensor(out=HV("P2", 0, T7), in0=kctl(),
                                      in1=bass.AP(arF, skr + T, [[AFC, 128], [1, T7]]), op=ALU.mult)
                    return eng.tensor_tensor(out=HV("Q2", 0, T7), in0=kctl(),
                                             in1=bass.AP(arF, skr + 2048 + T, [[AFC, 128], [1, T7]]),
                                             op=ALU.mult)
                ph_pqK = vph("epqK", f_pqK, after={"vector": ph_adK})
                def f_blend(eng, ctx, s=sqr, dd=dqr):
                    eng.scalar_tensor_tensor(out=bass.AP(arF, dd, [[AFC, 128], [1, T7]]),
                                             in0=PS[:, 0:T7], scalar=1.0,
                                             in1=bass.AP(arF, s + T, [[AFC, 128], [1, T7]]),
                                             op0=ALU.mult, op1=ALU.add)
                    return eng.scalar_tensor_tensor(out=bass.AP(arF, dd + 2048, [[AFC, 128], [1, T7]]),
                                                    in0=PS[:, 2048:2048 + T7], scalar=1.0,
                                                    in1=bass.AP(arF, s + 2048 + T, [[AFC, 128], [1, T7]]),
                                                    op0=ALU.mult, op1=ALU.add)
                ph_blQ = vph("eblQ", f_blend, after={"tensor": ph_mmQ})
                ph_mmK = tph("emmK", lambda eng, ctx, d=d: f_mm(eng, ctx, d=d, pn="P2", qn="Q2"),
                             after={"vector": ph_blQ})
                def f_blendK(eng, ctx, s=skr, dd=dkr):
                    return f_blend(eng, ctx, s=s, dd=dd)
                ph_blK = vph("eblK", f_blendK, after={"tensor": ph_mmK})
                if d == 0:
                    def f_h7Q(eng, ctx, s=sqr, dd=dqr):
                        return eng.copy(out=bass.AP(arF, dd + T7, [[AFC, 128], [2048, 2], [1, T]]),
                                        in_=bass.AP(arF, s + T7, [[AFC, 128], [2048, 2], [1, T]]))
                    ph_h7Q = sph("eh7Q", f_h7Q, after={"scalar": ph_sqK})
                    def f_h7K(eng, ctx, s=skr, dd=dkr):
                        return eng.copy(out=bass.AP(arF, dd + T7, [[AFC, 128], [2048, 2], [1, T]]),
                                        in_=bass.AP(arF, s + T7, [[AFC, 128], [2048, 2], [1, T]]))
                    ph_h7K = sph("eh7K", f_h7K, after={"scalar": ph_h7Q})
            ph_entend = ph_blK
            oA, oB = regs["stA"][0], regs["stB"][0]
            oAk, oBk = regs["stAk"][0], regs["stBk"][0]
            def f_trQ(eng, ctx):
                eng.copy(out=bass.AP(arF, oA, [[AFC, 128], [1, 8], [16, T]]),
                         in_=bass.AP(arF, oB, [[AFC, 128], [T, 8], [1, T]]))
                return eng.copy(out=bass.AP(arF, oA + 8, [[AFC, 128], [1, 8], [16, T]]),
                                in_=bass.AP(arF, oB + 2048, [[AFC, 128], [T, 8], [1, T]]))
            ph_trQ = sph("etrQ", f_trQ, after={"scalar": ph_h7K, "vector": ph_blQ})
            def f_trK(eng, ctx):
                eng.tensor_copy(out=bass.AP(arF, oAk, [[AFC, 128], [1, 8], [16, T]]),
                                in_=bass.AP(arF, oBk, [[AFC, 128], [T, 8], [1, T]]))
                return eng.tensor_copy(out=bass.AP(arF, oAk + 8, [[AFC, 128], [1, 8], [16, T]]),
                                       in_=bass.AP(arF, oBk + 2048, [[AFC, 128], [T, 8], [1, T]]))
            ph_trK = vph("etrK", f_trK, after={"vector": ph_blK, "scalar": ph_h7K})
            ph_kr1 = vph("ekr1", lambda eng, ctx: eng.tensor_copy(
                out=bass.AP(arF, oB, [[AFC, 128], [1, 8], [16, T]]),
                in_=bass.AP(arF, oBk + 2048, [[AFC, 128], [T, 8], [1, T]])),
                after={"vector": ph_trK, "scalar": ph_trQ})
            ph_kr2 = sph("ekr2", lambda eng, ctx: eng.activation(
                out=bass.AP(arF, oB + 8, [[AFC, 128], [1, 8], [16, T]]),
                in_=bass.AP(arF, oBk, [[AFC, 128], [T, 8], [1, T]]),
                func=AF.Copy, scale=-1.0),
                after={"scalar": ph_trQ, "vector": ph_blK})
            tapdump("stB", lambda: V("stA"), ti)

            def f_pat(eng, ctx):
                last = None
                for t in range(T):
                    last = eng.matmul(PS[0:8, t * 16: t * 16 + 16],
                                      V("stA", t * 16, 8), V("stAk", t * 16, 16),
                                      start=(t % 32 == 0), stop=False)
                    last = eng.matmul(PS[0:8, t * 16: t * 16 + 16],
                                      V("stA", t * 16 + 8, 8), V("stB", t * 16, 16),
                                      start=False, stop=(t % 32 == 31))
                return last
            ph_pat = tph("patm", f_pat, after={"scalar": ph_kr2, "vector": ph_kr1})
            Rre = lambda: PS[0:8, 0:16 * T].rearrange("p (t c j) -> p t c j", c=2, j=8)[:, :, 0, :]
            Rim = lambda: PS[0:8, 0:16 * T].rearrange("p (t c j) -> p t c j", c=2, j=8)[:, :, 1, :]

            S0v = lambda: V("stB", 0, T8)[0:8, :]
            EEv = lambda: V("stB", T8, T8)[0:32, :]
            PRv = lambda: V("stBk", 0, T8)[0:32, :]
            Wv = lambda: zsq[0:32, T8:2 * T8]
            M0v = lambda: V("qbR", 1024 + 2 * T8, T)[0:8, :]
            RSv = lambda: V("qbR", 1024 + 2 * T8 + T, T)[0:32, :]
            RS2v = lambda: V("qbR", 1024 + 2 * T8 + 2 * T, T)[0:32, :]
            def f_s0(eng, ctx):
                eng.activation(out=S0v(), in_=Rre(), func=AF.Square)
                return eng.activation(out=EEv()[0:8, :], in_=Rim(), func=AF.Square)
            ph_s0 = sph("smS0", f_s0, after={"tensor": ph_pat})
            ph_d1 = vph("smD1", lambda eng, ctx: eng.tensor_tensor(
                out=PRv()[0:8, :], in0=S0v(), in1=EEv()[0:8, :], op=ALU.add),
                after={"scalar": ph_s0})
            ph_d2 = vph("smD2", lambda eng, ctx: eng.tensor_reduce(
                out=M0v(), in_=PRv()[0:8, :].rearrange("p (t j) -> p t j", j=8), axis=AX.X, op=ALU.max),
                after={"vector": ph_d1})
            ph_d3 = vph("smD3", lambda eng, ctx: eng.tensor_tensor(
                out=zsq[0:8, T8:2 * T8].rearrange("p (t j) -> p t j", j=8),
                in0=PRv()[0:8, :].rearrange("p (t j) -> p t j", j=8),
                in1=M0v().unsqueeze(2).broadcast_to([8, T, 8]), op=ALU.subtract),
                after={"vector": ph_d2})
            Dv = lambda: zsq[0:8, T8:2 * T8]
            def f_ds(eng, ctx):
                last = None
                for q4 in range(T8 // 512):
                    last = eng.matmul(PS[0:32, q4 * 512:q4 * 512 + 512], e4R[:],
                                      Dv()[0:8, q4 * 512:(q4 + 1) * 512], start=True, stop=True)
                return last
            ph_ds = tph("smDS", f_ds, after={"vector": ph_d3, "scalar": ph_s0})
            ph_e = sph("smE", lambda eng, ctx: eng.activation(out=EEv(), in_=PS[0:32, 0:T8], func=AF.Exp),
                       after={"tensor": ph_ds})
            ph_p1 = vph("smP1", lambda eng, ctx: eng.tensor_reduce(
                out=RSv(), in_=EEv().rearrange("p (t j) -> p t j", j=8), axis=AX.X, op=ALU.add),
                after={"scalar": ph_e})
            ph_p2a = sph("smP2a", lambda eng, ctx: eng.activation(out=RS2v(), in_=RSv(), func=AF.Ln),
                         after={"vector": ph_p1})
            ph_p2 = sph("smP2b", lambda eng, ctx: eng.activation(out=RSv(), in_=RS2v(),
                                                                 func=AF.Exp, scale=-1.0),
                        after={"scalar": ph_p2a})
            ph_p3 = vph("smP3", lambda eng, ctx: eng.tensor_tensor(
                out=PRv().rearrange("p (t j) -> p t j", j=8),
                in0=EEv().rearrange("p (t j) -> p t j", j=8),
                in1=RSv().unsqueeze(2).broadcast_to([32, T, 8]), op=ALU.mult),
                after={"scalar": ph_p2})
            ph_w2 = sph("smW", lambda eng, ctx: eng.activation(out=Wv(), in_=PRv(), func=AF.Sqrt,
                                                               bias=kbias[0:32, 0:1]),
                        after={"vector": ph_p3})
            tapdump("W", lambda: f32(Wv()), ti)

            vsre = lambda: V("aux", 0, T)
            vsim = lambda: V("aux", T, T)
            cohre = lambda: V("x_tm", 640, T)
            cohim = lambda: V("x_tm", 896, T)
            cs = lambda i: V("qbR", i * 512, 512)
            def f_c1(eng, ctx):
                vev = f32(zve[:])
                eng.tensor_reduce(out=vsre(), in_=vev[:, 0:T8].rearrange("p (t j) -> p t j", j=8),
                                  axis=AX.X, op=ALU.add)
                return eng.tensor_reduce(out=vsim(), in_=vev[:, T8:2 * T8].rearrange("p (t j) -> p t j", j=8),
                                         axis=AX.X, op=ALU.add)
            ph_c1 = vph("coh1", f_c1, after={"vector": ph_entend})
            def f_c2(eng, ctx):
                eng.activation(out=cs(0)[:, 0:T], in_=vsre(), func=AF.Square)
                return eng.activation(out=cs(1)[:, 0:T], in_=vsim(), func=AF.Square)
            ph_c2 = sph("coh2", f_c2, after={"vector": ph_c1})
            ph_c3a = vph("coh3a", lambda eng, ctx: eng.tensor_tensor(
                out=cs(2)[:, 0:T], in0=cs(0)[:, 0:T], in1=cs(1)[:, 0:T], op=ALU.add),
                after={"scalar": ph_c2})
            ph_c3b = vph("coh3b", lambda eng, ctx: eng.tensor_single_scalar(
                out=cs(3)[:, 0:T], in_=cs(2)[:, 0:T], scalar=1e-36, op=ALU.max),
                after={"vector": ph_c3a})
            ph_c4a = sph("coh4a", lambda eng, ctx: eng.activation(out=cs(0)[:, 0:T], in_=cs(3)[:, 0:T], func=AF.Ln),
                         after={"vector": ph_c3b})
            ph_c4b = sph("coh4b", lambda eng, ctx: eng.activation(out=cs(1)[:, 0:T], in_=cs(0)[:, 0:T],
                                                                  func=AF.Exp, scale=-0.5),
                         after={"scalar": ph_c4a})
            def f_c5(eng, ctx):
                eng.tensor_tensor(out=cohre(), in0=vsre(), in1=cs(1)[:, 0:T], op=ALU.mult)
                return eng.tensor_tensor(out=cohim(), in0=vsim(), in1=cs(1)[:, 0:T], op=ALU.mult)
            ph_c5 = vph("coh5", f_c5, after={"scalar": ph_c4b, "sem": [(osem, 16 * no)]})

            msFre = lambda: V("stA", 0, T8)
            msFim = lambda: V("stA", T8, T8)
            FA = (lambda: zsq[:, 5120:5632], lambda: zsq[:, 5632:6144])
            FB = (lambda: zsq[:, 0:512], lambda: zsq[:, 512:1024])
            Frev, Fimv = FB
            vqTv = lambda: zsq[:, 4096:5120]
            wT4v = lambda: V("x_tm", 512, 128)
            statRv = lambda: zsq[:, 0:2048]
            ph_vqte_prev = None
            ph_fev = None
            ph_clq_last = None
            ph_g3_prev = None
            prep = {}

            def emit_prep(r):
                def f_vqt(eng, ctx, r=r):
                    last = None
                    for gl in range(4):
                        g = r * 4 + gl
                        for par in range(2):
                            last = eng.transpose(PS[:, (gl * 2 + par) * 128:(gl * 2 + par) * 128 + 128].bitcast(F32R),
                                                 zve[:, par * T8 + g * 128: par * T8 + (g + 1) * 128], eyeR[:])
                    return last
                vqt_dep = ({"scalar": ph_e, "vector": ph_c1} if r == 0
                           else {"scalar": ph_clq_last, "vector": ph_g3_prev})
                prep["vqt"] = tph("vqt", f_vqt, after=vqt_dep)
                def f_vqe2(eng, ctx):
                    last = None
                    for gl in range(4):
                        last = eng.copy(out=zsq[:, 4096 + gl * 256:4096 + gl * 256 + 128],
                                        in_=PS[:, (gl * 2) * 128:(gl * 2) * 128 + 128].bitcast(F32R))
                        last = eng.copy(out=zsq[:, 4096 + gl * 256 + 128:4096 + gl * 256 + 256],
                                        in_=PS[:, (gl * 2 + 1) * 128:(gl * 2 + 1) * 128 + 128].bitcast(F32R))
                    return last
                prep["vqte"] = sph("vqte", f_vqe2, after={"tensor": prep["vqt"], "scalar": ph_w2})
                def f_wtt(eng, ctx, r=r):
                    last = None
                    for gl in range(4):
                        last = eng.transpose(PS[:, 2048 + gl * 32: 2048 + gl * 32 + 32].bitcast(F32R),
                                             Wv()[0:32, (r * 4 + gl) * 128:(r * 4 + gl) * 128 + 128],
                                             eyeR[0:32, 0:32])
                    return last
                prep["wtt"] = tph("wtt", f_wtt, after={"scalar": prep["vqte"]})
                prep["wte"] = vph("wtte", lambda eng, ctx: eng.tensor_copy(out=wT4v(), in_=PS[:, 2048:2048 + 128]),
                                  after={"tensor": prep["wtt"], "vector": ph_c5})

            def emit_statb(extra):
                def f_stb(eng, ctx):
                    owT = regs["x_tm"][0] + 512
                    last = None
                    for c in range(4):
                        i0 = bass.AP(arF, owT + c * 8, [[AFC, 128], [32, 4], [0, 16], [1, 8]])
                        i1 = bass.AP(bmask, 0, [[128, 128], [0, 4], [8, 16], [1, 8]])
                        o2 = bass.AP(zsq, c * 512, [[24 * T, 128], [128, 4], [8, 16], [1, 8]])
                        last = eng.tensor_tensor(out=o2, in0=i0, in1=i1, op=ALU.mult)
                    return last
                stb_dep = {"vector": prep["wte"]}
                stb_dep.update(extra)
                prep["statb"] = gph("statb", f_stb, after=stb_dep)

            emit_prep(0)
            emit_statb({})
            for r in range(NR):
                ph_vqte = prep["vqte"]
                ph_statb = prep["statb"]
                def f_wmm(eng, ctx, chalf=0):
                    last = None
                    for c in (chalf * 2, chalf * 2 + 1):
                        for gl in range(4):
                            bank_col = c * 1024 + gl * 256
                            last = eng.matmul(PS[:, bank_col:bank_col + 256],
                                              statRv()[:, (c * 4 + gl) * 128:(c * 4 + gl) * 128 + 128],
                                              vqTv()[:, gl * 256:(gl + 1) * 256],
                                              start=(gl % 2 == 0), stop=(gl % 2 == 1))
                    return last
                wmm_dep = {"gpsimd": ph_statb, "scalar": ph_vqte if ph_fev is None else ph_fev}
                ph_wmm_a = tph("wmmA", lambda eng, ctx: f_wmm(eng, ctx, 0), after=wmm_dep)
                ph_wmm = tph("wmmB", lambda eng, ctx: f_wmm(eng, ctx, 1), after=wmm_dep)
                ps4all = lambda: PS[:, 0:4096].rearrange("p (c g two d) -> p c g two d", c=4, two=2, d=128)
                ps3all = lambda: PS[:, 0:4096].rearrange("p (cg two d) -> p cg two d", two=2, d=128)
                Acl = lambda o, n: V("qbR", o, n)
                def f_sq2(eng, ctx, chalf=0):
                    eng.activation(out=Acl(chalf * 1024, 1024).rearrange("p (cg d) -> p cg d", d=128),
                                   in_=ps3all()[:, chalf * 8:chalf * 8 + 8, 0, :], func=AF.Square)
                    return eng.activation(out=Acl(2048 + chalf * 1024, 1024).rearrange("p (cg d) -> p cg d", d=128),
                                          in_=ps3all()[:, chalf * 8:chalf * 8 + 8, 1, :], func=AF.Square)
                ph_sq_a = sph("clq2a", lambda eng, ctx: f_sq2(eng, ctx, 0),
                              after={"tensor": ph_wmm_a, "gpsimd": ph_statb})
                ph_sqim = sph("clq2b", lambda eng, ctx: f_sq2(eng, ctx, 1),
                              after={"tensor": ph_wmm, "scalar": ph_sq_a})
                ph_add = vph("cladd", lambda eng, ctx: eng.tensor_tensor(
                    out=Acl(4096, 2048), in0=Acl(0, 2048), in1=Acl(2048, 2048), op=ALU.add),
                    after={"scalar": ph_sqim})
                ph_ln = sph("clln", lambda eng, ctx: eng.activation(
                    out=Acl(0, 2048), in_=Acl(4096, 2048), func=AF.Ln, bias=kbias[:, 2:3]),
                    after={"vector": ph_add})
                ph_rsq = sph("clex", lambda eng, ctx: eng.activation(
                    out=Acl(2048, 2048), in_=Acl(0, 2048), func=AF.Exp, scale=-0.5),
                    after={"scalar": ph_ln})
                ph_g1 = vph("clg1", lambda eng, ctx: eng.tensor_scalar(
                    out=Acl(4096, 2048), in0=Acl(2048, 2048), scalar1=1e-6, scalar2=1.0,
                    op0=ALU.mult, op1=ALU.max), after={"scalar": ph_rsq})
                ph_g2 = vph("clg2", lambda eng, ctx: eng.tensor_tensor(
                    out=Acl(0, 2048), in0=Acl(2048, 2048), in1=Acl(4096, 2048), op=ALU.min),
                    after={"vector": ph_g1})
                g3dst = [(FB[0](), FB[1]()), (zsq[:, 5120:5632], zsq[:, 5632:6144]),
                         (Acl(2048, 512), Acl(2560, 512)), (Acl(3072, 512), Acl(3584, 512))]
                def f_g3(eng, ctx):
                    last = None
                    for c in range(4):
                        fc = Acl(c * 512, 512).rearrange("p (g d) -> p g d", d=128)
                        for par in range(2):
                            last = eng.scalar_tensor_tensor(
                                out=g3dst[c][par].rearrange("p (g d) -> p g d", d=128),
                                in0=fc, scalar=supw_sb[:, c:c + 1],
                                in1=ps4all()[:, c, :, par, :], op0=ALU.mult, op1=ALU.mult)
                    return last
                ph_g3 = vph("clg3", f_g3, after={"vector": ph_g2})
                ph_clq_last = ph_sqim
                ph_g3_prev = ph_g3
                if r + 1 < NR:
                    emit_prep(r + 1)
                def f_acc1(eng, ctx):
                    eng.tensor_tensor(out=Acl(4096, 1024), in0=f32(zsq[:, 0:1024]),
                                      in1=f32(zsq[:, 5120:6144]), op=ALU.add)
                    return eng.tensor_tensor(out=Acl(5120, 1024), in0=Acl(2048, 1024),
                                             in1=Acl(3072, 1024), op=ALU.add)
                ph_acc1 = vph("clac1", f_acc1, after={"vector": ph_g3})
                ph_acc2 = vph("clac2", lambda eng, ctx: eng.tensor_tensor(
                    out=zsq[:, 0:1024], in0=Acl(4096, 1024), in1=Acl(5120, 1024), op=ALU.add),
                    after={"vector": ph_acc1})
                prev_acc = ph_acc2
                ph_clq_last2v = prev_acc
                def f_ftr(eng, ctx):
                    last = None
                    for par, bufv in ((0, Frev), (1, Fimv)):
                        for gl in range(4):
                            last = eng.transpose(
                                PS[:, 1024 + par * 512 + gl * 128: 1024 + par * 512 + (gl + 1) * 128].bitcast(F32R),
                                bufv()[:, gl * 128:(gl + 1) * 128], eyeR[:])
                    return last
                ph_ftr = tph("ftr", f_ftr, after={"vector": prev_acc, "scalar": ph_clq_last})
                if r + 1 < NR:
                    emit_statb({"tensor": ph_ftr})
                def f_fev(eng, ctx, r=r):
                    oRe = regs["stA"][0]
                    oIm = regs["stA"][0] + T8
                    last = None
                    for par, oo in ((0, oRe), (1, oIm)):
                        o = bass.AP(arF, oo + r * 64, [[AFC, 128], [16, 4], [1, 16], [T, 8]])
                        last = eng.copy(out=o, in_=PS[:, 1024 + par * 512:1024 + (par + 1) * 512]
                                        .rearrange("p (g t i) -> p g t i", t=16, i=8))
                    return last
                ph_fev = sph("fev", f_fev, after={"tensor": ph_ftr})
                ph_vqte_prev = ph_vqte
            tapdump("msF", msFre, ti)

            def f_meas1(eng, ctx):
                chb_re = cohre().unsqueeze(1).broadcast_to([128, 8, T])
                chb_im = cohim().unsqueeze(1).broadcast_to([128, 8, T])
                eng.tensor_tensor(out=zsq[:, T8:2 * T8].rearrange("p (i t) -> p i t", i=8),
                                  in0=msFre().rearrange("p (i t) -> p i t", i=8), in1=chb_re, op=ALU.mult)
                return eng.tensor_tensor(out=zsq[:, 2 * T8:3 * T8].rearrange("p (i t) -> p i t", i=8),
                                         in0=msFim().rearrange("p (i t) -> p i t", i=8), in1=chb_im, op=ALU.mult)
            ph_m1 = vph("measv1", f_meas1, after={"scalar": ph_fev, "vector": ph_c5})
            ph_m2 = vph("measv2", lambda eng, ctx: eng.tensor_tensor(
                out=HV("meas", 0, T8), in0=f32(zsq[:, T8:2 * T8]), in1=f32(zsq[:, 2 * T8:3 * T8]),
                op=ALU.subtract), after={"vector": ph_m1})
            tapdump("meas", lambda: f32(zsq[:, T8:2 * T8]), ti)

            if TH_n is not None:
                part1deps[ti + 1]["xl"] = {"vector": ph_clq_last2v}
                part1deps[ti + 1]["xtr_sc"] = ph_fev
                part1deps[ti + 1]["xtr_v"] = ph_clq_last2v
                TH_n["load"]()

            ph_woe = {}
            ph_wom = {}
            for g in range(2):
                sl = nfetch % 2
                nfetch += 1
                def f_wow(eng, ctx, g=g, sl=sl):
                    return ctx.dma(wslot[:, sl * 4096:sl * 4096 + 4096], s_wout[:, g * 4096:(g + 1) * 4096])
                ph_wow = dph("woutw", f_wow, 1, after={"tensor": slot_user[sl]}, sem=wsem[sl])
                nwv[sl] += 1
                def f_wom(eng, ctx, g=g, sl=sl):
                    last = None
                    pc = 2048 + (g % 2) * 1024
                    for ml in range(4):
                        for k in range(8):
                            last = eng.matmul(PS[:, pc + ml * 256:pc + ml * 256 + T],
                                              wslot[:, sl * 4096 + (ml * 8 + k) * 128:sl * 4096 + (ml * 8 + k) * 128 + 128],
                                              HV("meas", k * T, T), start=(k == 0), stop=(k == 7))
                    return last
                womdep = {"sem": [(wsem[sl], 16 * nwv[sl])], "vector": ph_m2, "scalar": ph_fev}
                ph_wom[g] = tph("woutm", f_wom, after=womdep)
                slot_user[sl] = ph_wom[g]
                def f_woe(eng, ctx, g=g):
                    last = None
                    pc = 2048 + (g % 2) * 1024
                    for ml in range(4):
                        c = g * 4 + ml
                        last = eng.scalar_tensor_tensor(out=zve[:, c * T:(c + 1) * T],
                                                        in0=PS[:, pc + ml * 256:pc + ml * 256 + T],
                                                        scalar=bout_sb[:, c:c + 1],
                                                        in1=f32(zxf[:, c * T:(c + 1) * T]),
                                                        op0=ALU.add, op1=ALU.add)
                    return last
                ph_woe[g] = vph("woute", f_woe, after={"tensor": ph_wom[g]})
                if TH_n is not None:
                    part1deps[ti + 1][f"xe{g}_v"] = ph_woe[g]
                    if g == 0:
                        TH_n["xtr0"]()
                    else:
                        TH_n["xtr1"]()
                        TH_n["cast"]()
            x1v = lambda: zve[:, 0:T8]
            x2v = lambda: zve[:, T8:2 * T8]
            tapdump("x1", lambda: f32(x1v()), ti)

            def layernorm(srcv, dstv, bf_name, dep0, dep0a=None):
                def f_lmm_half(eng, ctx, ks=0, ke=8):
                    last = None
                    for k in range(ks, ke):
                        last = eng.matmul(PS[0:1, 0:T], onescR[:],
                                          srcv()[:, k * T:(k + 1) * T], start=(k == 0), stop=(k == 7))
                    for k in range(ks, ke):
                        last = eng.matmul(PS[0:1, 512:512 + T], onescR[:],
                                          zsq[:, k * T:(k + 1) * T], start=(k == 0), stop=(k == 7))
                    return last
                if dep0a is not None:
                    ph_lsq_a = sph("lnsqa", lambda eng, ctx: eng.activation(
                        out=zsq[:, 0:4 * T], in_=f32(srcv()[:, 0:4 * T]), func=AF.Square), after=dep0a)
                    ph_lmm_a = tph("lnmma", lambda eng, ctx: f_lmm_half(eng, ctx, 0, 4),
                                   after={"scalar": ph_lsq_a})
                    ph_lsq = sph("lnsqb", lambda eng, ctx: eng.activation(
                        out=zsq[:, 4 * T:T8], in_=f32(srcv()[:, 4 * T:T8]), func=AF.Square), after=dep0)
                    ph_lmm = tph("lnmmb", lambda eng, ctx: f_lmm_half(eng, ctx, 4, 8),
                                 after={"scalar": ph_lsq})
                else:
                    ph_lsq = sph("lnsq", lambda eng, ctx: eng.activation(out=zsq[:, 0:T8], in_=f32(srcv()),
                                                                         func=AF.Square), after=dep0)
                    ph_lmm = tph("lnmm", lambda eng, ctx: f_lmm_half(eng, ctx, 0, 8),
                                 after={"scalar": ph_lsq})
                ph_l1 = vph("lnth1", lambda eng, ctx: (
                    eng.tensor_single_scalar(out=thin[0:1, 0:T], in_=PS[0:1, 0:T],
                                             scalar=1.0 / 1024.0, op=ALU.mult),
                    eng.tensor_single_scalar(out=thin[0:1, 256:256 + T],
                                             in_=PS[0:1, 512:512 + T],
                                             scalar=1.0 / 1024.0, op=ALU.mult))[-1],
                    after={"tensor": ph_lmm})
                ph_l2 = vph("lnth2", lambda eng, ctx: eng.tensor_tensor(
                    out=thin[0:1, 512:512 + T], in0=thin[0:1, 0:T],
                    in1=thin[0:1, 0:T], op=ALU.mult), after={"vector": ph_l1})
                ph_l3 = vph("lnth3", lambda eng, ctx: eng.tensor_tensor(
                    out=thin[0:1, 768:768 + T], in0=thin[0:1, 256:256 + T],
                    in1=thin[0:1, 512:512 + T], op=ALU.subtract), after={"vector": ph_l2})
                ph_r1 = sph("lnrs1", lambda eng, ctx: eng.activation(
                    out=thin[0:1, 1024:1024 + T], in_=thin[0:1, 768:768 + T],
                    func=AF.Ln, bias=kbias[0:1, 1:2]), after={"vector": ph_l3})
                ph_r2 = sph("lnrs2", lambda eng, ctx: eng.activation(
                    out=thin[0:1, 512:512 + T], in_=thin[0:1, 1024:1024 + T], func=AF.Exp, scale=-0.5),
                    after={"scalar": ph_r1})
                ph_m1x = vph("lnmr1", lambda eng, ctx: (
                    eng.tensor_copy(out=zsq[0:1, 0:T], in_=thin[0:1, 512:512 + T]),
                    eng.scalar_tensor_tensor(out=thin[0:1, 768:768 + T], in0=thin[0:1, 0:T],
                                             scalar=-1.0, in1=thin[0:1, 512:512 + T],
                                             op0=ALU.mult, op1=ALU.mult))[-1],
                    after={"scalar": ph_r2})
                ph_m2x = vph("lnmr2", lambda eng, ctx: eng.tensor_copy(
                    out=zsq[0:1, 256:256 + T], in_=thin[0:1, 768:768 + T]),
                    after={"vector": ph_m1x})
                def f_lbc(eng, ctx):
                    eng.matmul(PS[:, 3072:3072 + T], onesrR[:], zsq[0:1, 0:T], start=True, stop=True)
                    return eng.matmul(PS[:, 3584:3584 + T], onesrR[:], zsq[0:1, 256:256 + T],
                                      start=True, stop=True)
                ph_lbc = tph("lnbc", f_lbc, after={"vector": ph_m2x})
                ph_a1 = vph("lnap1", lambda eng, ctx: eng.tensor_tensor(
                    out=zsq[:, 0:T8].rearrange("p (k t) -> p k t", k=8),
                    in0=f32(srcv()).rearrange("p (k t) -> p k t", k=8),
                    in1=PS[:, 3072:3072 + T].unsqueeze(1).broadcast_to([128, 8, T]), op=ALU.mult),
                    after={"tensor": ph_lbc})
                ph_a2 = vph("lnap2", lambda eng, ctx: eng.tensor_tensor(
                    out=dstv().rearrange("p (k t) -> p k t", k=8),
                    in0=f32(zsq[:, 0:T8]).rearrange("p (k t) -> p k t", k=8),
                    in1=PS[:, 3584:3584 + T].unsqueeze(1).broadcast_to([128, 8, T]), op=ALU.add),
                    after={"vector": ph_a1})
                ph_lc = None
                if bf_name:
                    ph_lc = sph("lncast", lambda eng, ctx: eng.copy(out=HV(bf_name, 0, T8), in_=f32(dstv())),
                                after={"vector": ph_a2})
                return ph_a2, ph_lc

            ph_ln1, ph_lc1 = layernorm(x1v, x2v, "x2bf", {"vector": ph_woe[1]},
                                       dep0a={"vector": ph_woe[0]})
            tapdump("x2", lambda: f32(x2v()), ti)

            ph_fsl = None
            ph_fh = None
            ph_fdm = None
            for hg in range(3):
              for gw in (hg * 2, hg * 2 + 1):
                sl = nfetch % 2
                nfetch += 1
                def f_fgw(eng, ctx, gw=gw, sl=sl):
                    return ctx.dma(wslot[:, sl * 4096:sl * 4096 + 4096], s_wg[:, gw * 4096:(gw + 1) * 4096])
                ph_fgw = dph("ffgw", f_fgw, 1, after={"tensor": slot_user[sl]}, sem=wsem[sl])
                nwv[sl] += 1
                fgm_dep = {"sem": [(wsem[sl], 16 * nwv[sl])], "scalar": ph_lc1 if ph_fsl is None else ph_fsl}
                if ph_fh is not None:
                    fgm_dep["vector"] = ph_fh
                def f_fgm(eng, ctx, sl=sl):
                    last = None
                    for ml in range(4):
                        for k in range(8):
                            last = eng.matmul(PS[:, ml * 256:ml * 256 + T],
                                              wslot[:, sl * 4096 + (ml * 8 + k) * 128:sl * 4096 + (ml * 8 + k) * 128 + 128],
                                              HV("x2bf", k * T, T), start=(k == 0), stop=(k == 7))
                    return last
                ph_fgm = tph("ffgm", f_fgm, after=fgm_dep)
                slot_user[sl] = ph_fgm
                def f_fsl(eng, ctx, gw=gw):
                    last = None
                    for ml in range(4):
                        last = eng.activation(out=HV("meas", (gw % 2) * 4 * T + ml * T, T),
                                              in_=PS[:, ml * 256:ml * 256 + T], func=AF.Silu, scale=INV127)
                    return last
                ph_fsl = sph("ffsl", f_fsl, after={"tensor": ph_fgm})
                sl = nfetch % 2
                nfetch += 1
                def f_fuw(eng, ctx, gw=gw, sl=sl):
                    return ctx.dma(wslot[:, sl * 4096:sl * 4096 + 4096], s_wu[:, gw * 4096:(gw + 1) * 4096])
                ph_fuw = dph("ffuw", f_fuw, 1, after={"tensor": slot_user[sl]}, sem=wsem[sl])
                nwv[sl] += 1
                fum_sw = [(wsem[sl], 16 * nwv[sl])]
                def f_fum(eng, ctx, sl=sl):
                    last = None
                    for ml in range(4):
                        for k in range(8):
                            last = eng.matmul(PS[:, ml * 256:ml * 256 + T],
                                              wslot[:, sl * 4096 + (ml * 8 + k) * 128:sl * 4096 + (ml * 8 + k) * 128 + 128],
                                              HV("x2bf", k * T, T), start=(k == 0), stop=(k == 7))
                    return last
                ph_fum = tph("ffum", f_fum, after={"sem": fum_sw, "scalar": ph_fsl})
                slot_user[sl] = ph_fum
                def f_fh(eng, ctx, gw=gw):
                    last = None
                    for ml in range(4):
                        last = eng.scalar_tensor_tensor(out=HV("hbuf", (gw % 2) * 4 * T + ml * T, T),
                                                        in0=PS[:, ml * 256:ml * 256 + T], scalar=INV127,
                                                        in1=HV("meas", (gw % 2) * 4 * T + ml * T, T),
                                                        op0=ALU.mult, op1=ALU.mult)
                    return last
                ph_fh = vph("ffh", f_fh, after={"tensor": ph_fum, "scalar": ph_fsl})
                sl = nfetch % 2
                nfetch += 1
                def f_fdw(eng, ctx, gw=gw, sl=sl):
                    return ctx.dma(wslot[:, sl * 4096:sl * 4096 + 4096], s_wd[:, gw * 4096:(gw + 1) * 4096])
                ph_fdw = dph("ffdw", f_fdw, 1, after={"tensor": slot_user[sl]}, sem=wsem[sl])
                nwv[sl] += 1
                fdm_sw = [(wsem[sl], 16 * nwv[sl])]
                def f_fdm(eng, ctx, gw=gw, sl=sl):
                    last = None
                    for ml in range(8):
                        for kl in range(4):
                            last = eng.matmul(PS[:, 2048 + ml * 256: 2048 + ml * 256 + T],
                                              wslot[:, sl * 4096 + (kl * 8 + ml) * 128:sl * 4096 + (kl * 8 + ml) * 128 + 128],
                                              HV("hbuf", (gw % 2) * 4 * T + kl * T, T),
                                              start=(gw == 0 and kl == 0 and ml % 2 == 0),
                                              stop=(gw == 5 and kl == 3 and ml % 2 == 1))
                    return last
                ph_fdm = tph("ffdm", f_fdm, after={"sem": fdm_sw, "vector": ph_fh})
                slot_user[sl] = ph_fdm
                if TH_n is not None:
                    TH_n["qkv"](gw)
            if TH_n is not None:
                TH_n["norm"]()
                TH_n["rnb"]()
            def f_fde(eng, ctx):
                last = None
                for ml in range(8):
                    last = eng.scalar_tensor_tensor(out=zsq[:, 2048 + ml * T:2048 + (ml + 1) * T],
                                                    in0=PS[:, 2048 + ml * 256:2048 + ml * 256 + T], scalar=INV127,
                                                    in1=f32(x2v()[:, ml * T:(ml + 1) * T]), op0=ALU.mult, op1=ALU.add)
                return last
            fde_dep = {"tensor": ph_fdm}
            if TH_n is not None:
                fde_dep["tensor"] = part1H[ti + 1]["nm"]
            ph_fde = vph("ffde", f_fde, after=fde_dep)
            x3rawv = lambda: zsq[:, 2048:4096]
            tapdump("x3raw", lambda: f32(zsq[:, 2048:4096]), ti)
            if TH_n is not None:
                part1deps[ti + 1]["rb3_v"] = ph_fde
                TH_n["rb3"]()

            x3v = lambda: V("stBk", 0, T8)
            ln2dep = {"vector": ph_fde}
            if TH_n is not None:
                ln2dep["tensor"] = part1H[ti + 1]["nm"]
            ph_ln2, _ = layernorm(x3rawv, x3v, None, ln2dep)
            tapdump("x3", x3v, ti)

            def f_otr(eng, ctx):
                last = None
                for s in range(nsub):
                    for k in range(8):
                        last = eng.transpose(PS[:, (s * 8 + k) * 128:(s * 8 + k) * 128 + 128],
                                             x3v()[:, k * T + s * 128: k * T + (s + 1) * 128], eyeF[:])
                return last
            otr_sc = part1H[ti + 1]["qs"][11] if TH_n is not None else ph_fsl
            prev_otr = tph("otr", f_otr, after={"vector": ph_ln2, "scalar": otr_sc})
            oev_dep = {"tensor": prev_otr, "sem": [(osem, 16 * no), (rsem, 16 * nr)]}
            prev_oev = vph("oev", lambda eng, ctx: eng.tensor_copy(out=V("x_tm"), in_=PS[:, 0:T8]),
                           after=oev_dep)
            def f_ost(eng, ctx, t0=t0):
                last = None
                for s in range(nsub):
                    last = ctx.dma(y_out[t0 + s * 128: t0 + (s + 1) * 128, :], V("x_tm", s * 1024, 1024))
                return last
            prev_ost = dph("ost", f_ost, nsub, after={"vector": prev_oev}, sem=osem)
            no += nsub

        ph = sch.phase("fin", after={"all": True, "sem": [(osem, 16 * no)]})
        def f_fin(eng, ctx):
            return eng.engine_nop()
        sch.on(ph, "vector", f_fin)

        with nc.Block() as block:
            totE, totD = sch.emit(block, esems, dsem)
        nins = sum(len(b.instructions) if hasattr(b, 'instructions') else 0 for b in [])
        print(f"[sched] phases={len(sch.phases)} totE={totE} totD={totD}")

    return nc


def kernel_inputs(inp, tokens, ncores=8):
    consts = host_consts()
    common = {
        "wqkv": np.ascontiguousarray(np.asarray(inp["L0_attn_W_qkv"], np.float32)),
        "bqkv": np.ascontiguousarray(np.asarray(inp["L0_attn_b_qkv"], np.float32)).reshape(24, 128),
        "wout": np.ascontiguousarray(np.asarray(inp["L0_attn_W_out"], np.float32)),
        "bout": np.ascontiguousarray(np.asarray(inp["L0_attn_b_out"], np.float32)).reshape(8, 128),
        "qwrow": np.ascontiguousarray(np.asarray(inp["L0_attn_quantum_weights"], np.float32)).reshape(1, 256),
        "pp": np.ascontiguousarray(np.asarray(inp["L0_attn_photonic_phases"], np.float32)),
        "gaterow": np.ascontiguousarray(np.asarray(inp["L0_attn_entanglement_gates"], np.float32)).reshape(1, 96),
        "logrow": np.ascontiguousarray(np.asarray(inp["L0_attn_superposition_logits"], np.float32)).reshape(1, 4),
        "wg": np.ascontiguousarray(np.asarray(inp["L0_ffn_w_gate"], np.int32)),
        "wu": np.ascontiguousarray(np.asarray(inp["L0_ffn_w_up"], np.int32)),
        "wd": np.ascontiguousarray(np.asarray(inp["L0_ffn_w_down"], np.int32)),
    }
    for k, v in consts.items():
        common[k] = v
    x = np.ascontiguousarray(np.asarray(inp["L0_attn_query"], dtype=np.float32))
    B, S, D = x.shape
    in_maps = []
    for i in range(ncores):
        m = dict(common)
        xi = x[i % B].reshape(S, D)[:tokens]
        m["x"] = np.ascontiguousarray(xi)
        in_maps.append(m)
    return in_maps


def run_dev(inp, tokens=4096, T=256, ncores=8, tap=None):
    nc = build_nc(tokens, T, tap=tap)
    in_maps = kernel_inputs(inp, tokens, ncores)
    res = run_bass_kernel_spmd(nc, in_maps, core_ids=list(range(ncores)))
    outs = np.stack([res.results[i]["y"] for i in range(ncores)], axis=0).astype(np.float32)
    dbgs = [res.results[i].get("dbg") for i in range(ncores)]
    return outs, dbgs


_NC_CACHE = {}


def kernel(**inputs):
    tokens, T, ncores = 4096, 256, 8
    key = (tokens, T)
    if key not in _NC_CACHE:
        _NC_CACHE[key] = build_nc(tokens, T)
    nc = _NC_CACHE[key]
    in_maps = kernel_inputs(inputs, tokens, ncores)
    res = run_bass_kernel_spmd(nc, in_maps, core_ids=list(range(ncores)))
    out = np.stack([res.results[i]["y"] for i in range(ncores)], axis=0).astype(np.float32)
    return out
